# version 13 (final)
# speedup vs baseline: 1.2584x; 1.0439x over previous
import numpy as np
import concourse.bass as bass
import concourse.mybir as mybir
from concourse.bass_utils import run_bass_kernel_spmd

F32 = mybir.dt.float32
BF16 = mybir.dt.bfloat16
AF = mybir.ActivationFunctionType
ALU = mybir.AluOpType
AX = mybir.AxisListType

NCORES = 8
B_PER_CORE = 2
T_SEQ = 2048
TOK = B_PER_CORE * T_SEQ
D = 1024
NTILE = TOK // 128
EPS = 1e-5


class Sem:

    def __init__(self, nc, name):
        self.h = nc.alloc_semaphore(name)
        self.n = 0

    def inc(self, instr, k=1):
        instr.then_inc(self.h, k)
        self.n += k
        return instr

    def tok(self):
        return self.n

    def wait(self, eng, tok=None):
        v = self.n if tok is None else tok
        if v > 0:
            eng.wait_ge(self.h, v)


_WRAP_OPS = {
    "tensor_copy", "tensor_tensor", "tensor_add", "tensor_sub", "tensor_mul",
    "tensor_max", "tensor_scalar", "tensor_single_scalar", "tensor_scalar_add",
    "tensor_scalar_sub", "tensor_scalar_mul", "tensor_scalar_max",
    "tensor_scalar_min", "tensor_relu", "tensor_reduce", "reduce_max",
    "reduce_sum", "scalar_tensor_tensor", "tensor_tensor_scan", "memset",
    "memzero", "bn_stats", "bn_aggr", "reciprocal", "select", "copy_predicated",
    "transpose", "max", "max_index", "stream_shuffle", "activation", "square",
    "sqrt", "mul", "copy", "add", "sign", "iota", "affine_select", "pool",
    "pool_avg", "pool_max",
}


class SerialEng:

    def __init__(self, eng, sem):
        self._eng = eng
        self._sem = sem

    def __getattr__(self, name):
        f = getattr(self._eng, name)
        if name not in _WRAP_OPS:
            return f
        sem = self._sem
        eng = self._eng

        def g(*a, **k):
            if sem.n > 0:
                eng.wait_ge(sem.h, sem.n)
            ins = f(*a, **k)
            sem.inc(ins)
            return ins
        return g


POOL_COLS = 106368


class Ctx:

    def __init__(self, nc, block):
        self.nc = nc
        self.block = block
        self.t = Sem(nc, "t")
        self.v = Sem(nc, "v")
        self.s = Sem(nc, "s")
        self.g = Sem(nc, "g")
        self.sb = {}
        self._nsem = 0
        self.pool = nc.alloc_sbuf_tensor("pool", [128, POOL_COLS], BF16)
        self._off = 0
        self._perm = 0
        self.peak = 0
        self.ps_banks = [nc.alloc_psum_tensor("psb%d" % k, [128, 512], F32)
                         for k in range(8)]
        self._ps_off = 0
        self.gst = [self.dsem("gst", swdge=True) for _ in range(4)]

    def dsem(self, name, swdge=False):
        if swdge:
            self._nsem += 1
            return Sem(self.nc, "dsw%d" % self._nsem)
        if not hasattr(self, "_dpool"):
            self._dpool = []
            self._didx = 0
        if len(self._dpool) < 60:
            self._nsem += 1
            s = Sem(self.nc, "d%d" % self._nsem)
            self._dpool.append(s)
            return s
        s = self._dpool[self._didx]
        self._didx = (self._didx + 1) % len(self._dpool)
        return s

    def sbuf(self, name, shape, dtype, perm=False):
        assert len(shape) == 2 and shape[0] <= 128
        cols = shape[1] * (2 if dtype == F32 else 1)
        o_al = (self._off + 15) & ~15
        self._off = o_al + cols
        self.peak = max(self.peak, self._off)
        assert self._off <= POOL_COLS, \
            "pool overflow at %s: %d > %d" % (name, self._off, POOL_COLS)
        if perm:
            self._perm = self._off
        ap = self.pool[0:shape[0], o_al:o_al + cols]
        if dtype == F32:
            ap = ap.bitcast(F32)
        self.sb[name] = ap
        return ap

    def psum(self, name, shape=(128, 512), dtype=F32):
        assert shape[0] <= 128 and shape[1] <= 512 and dtype == F32
        assert self._ps_off < 8, "out of PSUM banks at %s" % name
        b = self.ps_banks[self._ps_off]
        self._ps_off += 1
        return b

    def phase_reset(self):
        self._off = self._perm
        self._ps_off = 0

    def mark(self):
        return self._off

    def reset_to(self, m):
        self._off = m

    def V(self, v):
        return SerialEng(v, self.v)

    def S(self, s):
        return SerialEng(s, self.s)

    def G(self, g):
        return SerialEng(g, self.g)


def emit_phase_barrier(ctx, extra_dsems=()):
    block = ctx.block
    toks = (ctx.t.tok(), ctx.v.tok(), ctx.s.tok(), ctx.g.tok())
    for engf in (block.sync, block.gpsimd, block.tensor, block.vector, block.scalar):
        @engf
        def _(e, toks=toks):
            ctx.t.wait(e, toks[0])
            ctx.v.wait(e, toks[1])
            ctx.s.wait(e, toks[2])
            ctx.g.wait(e, toks[3])
            for ds in extra_dsems:
                ds.wait(e)


def gwait(eng, guard):
    if guard is not None:
        guard[0].wait(eng, guard[1])


def ln_stats_batch_finish(ctx, stats_all, mean_sb, var_sb, rstd_sb, tmp,
                          ntile=NTILE):
    block = ctx.block
    S3 = stats_all[0:128, 0:ntile * 12].rearrange("p (t c) -> p t c", c=12)
    @block.vector
    def _(v):
        v = ctx.V(v)
        m1 = S3[:, :, 1]; m2 = S3[:, :, 4]; m3 = S3[:, :, 7]; m4 = S3[:, :, 10]
        s1 = S3[:, :, 2]; s2 = S3[:, :, 5]; s3 = S3[:, :, 8]; s4 = S3[:, :, 11]
        t1 = tmp[:, 0:ntile]; t2 = tmp[:, ntile:2 * ntile]
        v.tensor_add(t1, m1, m2)
        v.tensor_add(t2, m3, m4)
        v.tensor_add(t1, t1, t2)
        v.tensor_scalar_mul(mean_sb[:, 0:ntile], t1, 0.25)
        v.tensor_add(t1, s1, s2)
        v.tensor_add(t2, s3, s4)
        v.tensor_add(t1, t1, t2)
        v.tensor_scalar_mul(t1, t1, 1.0 / 1024.0)
        v.tensor_mul(t2, m1, m1)
        v.tensor_mul(var_sb[:, 0:ntile], m2, m2)
        v.tensor_add(t2, t2, var_sb[:, 0:ntile])
        v.tensor_mul(var_sb[:, 0:ntile], m3, m3)
        v.tensor_add(t2, t2, var_sb[:, 0:ntile])
        v.tensor_mul(var_sb[:, 0:ntile], m4, m4)
        v.tensor_add(t2, t2, var_sb[:, 0:ntile])
        v.scalar_tensor_tensor(t1, t2, 0.25, t1, ALU.mult, ALU.add)
        v.tensor_mul(t2, mean_sb[:, 0:ntile], mean_sb[:, 0:ntile])
        v.tensor_sub(t1, t1, t2)
        v.tensor_scalar_add(var_sb[:, 0:ntile], t1, EPS)
    @block.scalar
    def _(s):
        s = ctx.S(s)
        ctx.v.wait(s)
        s.activation(rstd_sb[:, 0:ntile], var_sb[:, 0:ntile], AF.Ln)
        s.activation(rstd_sb[:, 0:ntile], rstd_sb[:, 0:ntile], AF.Exp, scale=-0.5)
    return ctx.s.tok()


def emit_stats_pass(ctx, src_hbm, mean_sb, var_sb, rstd_sb, xbuf, stats_sb,
                    ntile=NTILE):
    block = ctx.block
    ld = [ctx.dsem("ld"), ctx.dsem("ld")]
    v_toks = []
    for i in range(ntile):
        @block.sync
        def _(q, i=i):
            if i >= 2:
                ctx.v.wait(q, v_toks[i - 2])
            else:
                ctx.v.wait(q)
            ld[i % 2].inc(q.dma_start(out=xbuf[i % 2][:, 0:512],
                                      in_=src_hbm[i * 128:(i + 1) * 128, 0:512]), 16)
        @block.scalar
        def _(q, i=i):
            if i >= 2:
                ctx.v.wait(q, v_toks[i - 2])
            else:
                ctx.v.wait(q)
            ld[i % 2].inc(q.dma_start(out=xbuf[i % 2][:, 512:1024],
                                      in_=src_hbm[i * 128:(i + 1) * 128, 512:1024]), 16)
        @block.vector
        def _(v, i=i):
            v = ctx.V(v)
            ld[i % 2].wait(v)
            v.bn_stats(stats_sb[:, i * 12:i * 12 + 6], xbuf[i % 2][:, 0:512])
            v.bn_stats(stats_sb[:, i * 12 + 6:i * 12 + 12], xbuf[i % 2][:, 512:1024])
            v_toks.append(ctx.v.tok())
    return ln_stats_batch_finish(ctx, stats_sb, mean_sb, var_sb, rstd_sb,
                                 stats_sb[0:128, ntile * 12:ntile * 12 + 2 * ntile],
                                 ntile)


def emit_ln_pass(ctx, src_hbm, n_tm_hbm, mean_sb, var_sb, rstd_sb, xbuf, nbuf,
                 stats_sb, ntile=NTILE, chunk_cb=None, in_stats=None):
    block = ctx.block
    if in_stats is not None:
        s_tok = ln_stats_batch_finish(ctx, in_stats, mean_sb, var_sb, rstd_sb,
                                      in_stats[0:128, NTILE * 12:NTILE * 14], ntile)
    else:
        s_tok = emit_stats_pass(ctx, src_hbm, mean_sb, var_sb, rstd_sb, xbuf, stats_sb, ntile)
    ld = [ctx.dsem("ld"), ctx.dsem("ld")]
    st = [ctx.dsem("lnst", swdge=True), ctx.dsem("lnst", swdge=True)]
    v2 = []
    for i in range(ntile):
        @block.sync
        def _(q, i=i):
            if i < 2:
                ctx.v.wait(q)
            else:
                ctx.v.wait(q, v2[i - 2])
            ld[i % 2].inc(q.dma_start(out=xbuf[i % 2][:, 0:512],
                                      in_=src_hbm[i * 128:(i + 1) * 128, 0:512]), 16)
        @block.scalar
        def _(q, i=i):
            if i < 2:
                ctx.v.wait(q)
            else:
                ctx.v.wait(q, v2[i - 2])
            ld[i % 2].inc(q.dma_start(out=xbuf[i % 2][:, 512:1024],
                                      in_=src_hbm[i * 128:(i + 1) * 128, 512:1024]), 16)
        @block.vector
        def _(v, i=i):
            v = ctx.V(v)
            ld[i % 2].wait(v)
            ctx.s.wait(v, s_tok)
            if i >= 2:
                st[i % 2].wait(v)
            v.tensor_scalar(nbuf[i % 2][:], xbuf[i % 2][:],
                            mean_sb[:, i:i + 1], rstd_sb[:, i:i + 1],
                            ALU.subtract, ALU.mult)
            v2.append(ctx.v.tok())
        @block.gpsimd
        def _(g, i=i):
            ctx.v.wait(g, v2[i])
            st[i % 2].inc(g.dma_start(out=n_tm_hbm[i * 128:(i + 1) * 128, :],
                                      in_=nbuf[i % 2][:]), 16)
        if chunk_cb is not None and i % 4 == 3 and i // 4 < 2:
            chunk_cb(i // 4, st)
    return st


def emit_ln_residual_epilogue(ctx, y_hbm, res_hbm, out_hbm, mean_sb, var_sb,
                              rstd_sb, xbuf, ybuf, obuf, stats_sb,
                              stats_ready=False, out_stats=None):
    block = ctx.block
    if stats_ready:
        s_tok = ln_stats_batch_finish(ctx, stats_sb, mean_sb, var_sb, rstd_sb,
                                      stats_sb[0:128, NTILE * 12:NTILE * 14])
    else:
        s_tok = emit_stats_pass(ctx, y_hbm, mean_sb, var_sb, rstd_sb, ybuf, stats_sb)
    ld = [ctx.dsem("ld"), ctx.dsem("ld")]
    ldr = [ctx.dsem("ldr"), ctx.dsem("ldr")]
    st = [ctx.gst[0], ctx.gst[1]]
    v1tk = []
    gtk = [None, None]
    for i in range(NTILE):
        @block.sync
        def _(q, i=i):
            if i < 2:
                ctx.v.wait(q)
                ctx.g.wait(q)
            else:
                ctx.v.wait(q, v1tk[i - 2])
            ld[i % 2].inc(q.dma_start(out=ybuf[i % 2][:],
                                      in_=y_hbm[i * 128:(i + 1) * 128, :]), 16)
        @block.scalar
        def _(q, i=i):
            if i < 2:
                ctx.v.wait(q)
                ctx.g.wait(q)
            else:
                gwait(q, gtk[i % 2])
            ldr[i % 2].inc(q.dma_start(out=xbuf[i % 2][:],
                                       in_=res_hbm[i * 128:(i + 1) * 128, :]), 16)
        @block.vector
        def _(v, i=i):
            v = ctx.V(v)
            ld[i % 2].wait(v)
            ldr[i % 2].wait(v)
            ctx.s.wait(v, s_tok)
            if i >= 2:
                st[i % 2].wait(v)
            v.tensor_scalar(obuf[i % 2][:], ybuf[i % 2][:],
                            mean_sb[:, i:i + 1], rstd_sb[:, i:i + 1],
                            ALU.subtract, ALU.mult)
            v.tensor_add(obuf[i % 2][:], obuf[i % 2][:], xbuf[i % 2][:])
            gtk[i % 2] = (ctx.v, ctx.v.tok())
            if out_stats is not None:
                v.bn_stats(out_stats[:, i * 12:i * 12 + 6], obuf[i % 2][:, 0:512])
                v.bn_stats(out_stats[:, i * 12 + 6:i * 12 + 12], obuf[i % 2][:, 512:1024])
            v1tk.append(ctx.v.tok())
        @block.gpsimd
        def _(g, i=i):
            gwait(g, gtk[i % 2])
            st[i % 2].inc(g.dma_start(out=out_hbm[i * 128:(i + 1) * 128, :],
                                      in_=obuf[i % 2][:]), 16)
    return st



def emit_phaseB(ctx, x_hbm, w1_hbm, b1_hbm, w2_hbm, b2_hbm, out_hbm,
                scr_n, scr_y, nb_limit=16, do_main=True, do_epi=True,
                in_stats=None, out_stats=None):
    nc, block = ctx.nc, ctx.block
    xbuf = [ctx.sbuf("B_x0", [128, 1024], F32), ctx.sbuf("B_x1", [128, 1024], F32)]
    nbuf = [ctx.sbuf("B_n0", [128, 1024], BF16), ctx.sbuf("B_n1", [128, 1024], BF16)]
    ybuf = [ctx.sbuf("B_y0", [128, 1024], F32), ctx.sbuf("B_y1", [128, 1024], F32)]
    obuf = [ctx.sbuf("B_o0", [128, 1024], F32), ctx.sbuf("B_o1", [128, 1024], F32)]
    stats = ctx.sbuf("B_stats", [128, NTILE * 12 + 2 * NTILE], F32)
    mean = ctx.sbuf("B_mean", [128, NTILE], F32)
    var = ctx.sbuf("B_var", [128, NTILE], F32)
    rstd = ctx.sbuf("B_rstd", [128, NTILE], F32)
    w1b = ctx.sbuf("B_w1b", [128, 8 * 4096], BF16)
    w2b = ctx.sbuf("B_w2b", [128, 32 * 1024], BF16)
    b1t = ctx.sbuf("B_b1t", [32, 128], F32)
    b1c = ctx.sbuf("B_b1c", [128, 32], F32)
    b2bc = ctx.sbuf("B_b2bc", [128, 1024], F32)
    identf = ctx.sbuf("B_idf", [128, 128], F32)
    nT = [ctx.sbuf("B_nT0", [128, 8 * 256], BF16), ctx.sbuf("B_nT1", [128, 8 * 256], BF16)]
    NSL = 4
    ebuf = [ctx.sbuf("B_e%d" % k, [128, 256], F32) for k in range(NSL)]
    ubuf = [ctx.sbuf("B_u%d" % k, [128, 256], F32) for k in range(NSL)]
    tbuf = [ctx.sbuf("B_t%d" % k, [128, 256], F32) for k in range(NSL)]
    intS = [ctx.sbuf("B_i%d" % k, [128, 256], BF16) for k in range(NSL)]
    hps = [ctx.psum("B_hps0"),
           ctx.psum("B_hps1")]
    ops = [ctx.psum("B_ops%d" % k) for k in range(4)]
    tps = ctx.psum("B_tps")

    wsem = ctx.dsem("w", swdge=True)
    csem = ctx.dsem("const")
    @block.gpsimd
    def _(g):
        wsem.inc(g.dma_start(out=w1b[:].rearrange("p (dc f) -> p dc f", dc=8),
                             in_=w1_hbm.rearrange("(dc p) f -> p dc f", p=128)), 16)
        wsem.inc(g.dma_start(out=w2b[:].rearrange("p (ic c) -> p ic c", ic=32),
                             in_=w2_hbm.rearrange("(ic p) c -> p ic c", p=128)), 16)
    @block.sync
    def _(q):
        csem.inc(q.dma_start(out=b1t[:], in_=b1_hbm.rearrange("(c p) -> c p", p=128)), 16)
        csem.inc(q.dma_start(out=b2bc[:], in_=b2_hbm[None, :].partition_broadcast(128)), 16)
        csem.inc(q.dma_start(out=identf[:], in_=ctx.ident_hbm[:, :]), 16)
    @block.tensor
    def _(te):
        csem.wait(te)
        ctx.t.inc(te.transpose(tps[:, 0:32], b1t[:, :], identf[0:32, 0:32]))
    @block.vector
    def _(v):
        v = ctx.V(v)
        ctx.t.wait(v)
        v.tensor_copy(b1c[:], tps[:, 0:32])
    b1_tok = ctx.v.tok()

    nTs_early = []

    def _B_load_chunk(tc, stsems):
        if tc > 0:
            return
        for blk in (2 * tc, 2 * tc + 1):
            @block.sync
            def _(q, blk=blk, stsems=stsems):
                for p2 in range(2):
                    stsems[p2].wait(q)
                for dc in range(8):
                    nTs_B[blk % 2].inc(q.dma_start(
                        out=nT[blk % 2][:, dc * 256:(dc + 1) * 256],
                        in_=scr_n[blk * 256:(blk + 1) * 256,
                                  dc * 128:(dc + 1) * 128],
                        transpose=True), 16)
            nTs_early.append(blk)

    nTs_B = [ctx.dsem("nT"), ctx.dsem("nT")]
    n_st = emit_ln_pass(ctx, x_hbm, scr_n, mean, var, rstd, xbuf, nbuf, stats,
                        chunk_cb=_B_load_chunk, in_stats=in_stats)

    NB = nb_limit if do_main else 0
    nTs = nTs_B
    nT_free = [None] * (NB + 1)
    hps_ev = [0, 0]
    intS_done = [0] * NSL
    intS_free = [0] * NSL
    ops_ev = [0, 0, 0, 0]
    yst = [ctx.gst[2], ctx.gst[3]]
    ev_toks = []

    for blk in range(NB):
        r0 = blk * 256
        if blk not in nTs_early:
            @block.sync
            def _(q, blk=blk, r0=r0):
                if blk == 2:
                    for p in range(2):
                        n_st[p].wait(q)
                if blk >= 2 and nT_free[blk - 2] is not None:
                    ctx.t.wait(q, nT_free[blk - 2])
                for dc in range(8):
                    nTs[blk % 2].inc(q.dma_start(out=nT[blk % 2][:, dc * 256:(dc + 1) * 256],
                                                 in_=scr_n[r0:r0 + 256, dc * 128:(dc + 1) * 128],
                                                 transpose=True), 16)
        def emit_w2(dffc, blk=blk):
            sl = dffc % NSL
            @block.tensor
            def _(te, blk=blk, dffc=dffc, sl=sl):
                ctx.v.wait(te, intS_done[sl])
                if dffc == 0:
                    for k in range(4):
                        ctx.v.wait(te, ops_ev[k])
                for tt in range(2):
                    for ch in range(2):
                        ins = te.matmul(ops[tt * 2 + ch][:],
                                        intS[sl][:, tt * 128:(tt + 1) * 128],
                                        w2b[:, dffc * 1024 + ch * 512: dffc * 1024 + (ch + 1) * 512],
                                        start=(dffc == 0), stop=(dffc == 31))
                ctx.t.inc(ins)
            intS_free[sl] = ctx.t.tok()

        W2LAG = 3

        def emitB_front(dffc, blk=blk):
            p = dffc % 2
            sl = dffc % NSL
            @block.tensor
            def _(te, blk=blk, dffc=dffc, p=p):
                if blk == 0 and dffc == 0:
                    wsem.wait(te)
                    ctx.v.wait(te, b1_tok)
                nTs[blk % 2].wait(te)
                ctx.v.wait(te, hps_ev[p])
                for dc in range(8):
                    ins = te.matmul(hps[p][:, 0:256],
                                    w1b[:, dc * 4096 + dffc * 128: dc * 4096 + (dffc + 1) * 128],
                                    nT[blk % 2][:, dc * 256:(dc + 1) * 256],
                                    start=(dc == 0), stop=(dc == 7))
                ctx.t.inc(ins)
            mm_tok = ctx.t.tok()
            @block.vector
            def _(v, dffc=dffc, p=p, sl=sl, mm_tok=mm_tok):
                v = ctx.V(v)
                ctx.t.wait(v, mm_tok)
                v.tensor_scalar(ebuf[sl][:], hps[p][:, 0:256],
                                b1c[:, dffc:dffc + 1], None, ALU.add)
            e_tok = ctx.v.tok()
            hps_ev[p] = e_tok
            @block.gpsimd
            def _(g, sl=sl, e_tok=e_tok):
                g = ctx.G(g)
                ctx.v.wait(g, e_tok)
                g.tensor_mul(ubuf[sl][:], ebuf[sl][:], ebuf[sl][:])
                g.tensor_scalar(ubuf[sl][:], ubuf[sl][:], 0.044715, 1.0,
                                ALU.mult, ALU.add)
            u_tok = ctx.g.tok()
            @block.vector
            def _(v, sl=sl, u_tok=u_tok):
                v = ctx.V(v)
                ctx.g.wait(v, u_tok)
                v.tensor_mul(ubuf[sl][:], ebuf[sl][:], ubuf[sl][:])
            return ctx.v.tok()

        def emitB_tail(dffc):
            sl = dffc % NSL
            @block.gpsimd
            def _(g, sl=sl):
                g = ctx.G(g)
                ctx.s.wait(g)
                g.tensor_scalar(tbuf[sl][:], tbuf[sl][:], 0.5, 0.5,
                                ALU.mult, ALU.add)
            s2_tok = ctx.g.tok()
            @block.vector
            def _(v, sl=sl, s2_tok=s2_tok):
                v = ctx.V(v)
                ctx.g.wait(v, s2_tok)
                ctx.t.wait(v, intS_free[sl])
                v.tensor_mul(intS[sl][:], ebuf[sl][:], tbuf[sl][:])
            intS_done[sl] = ctx.v.tok()

        for pr in range(16):
            d0, d1 = 2 * pr, 2 * pr + 1
            t0_ = emitB_front(d0)
            t1_ = emitB_front(d1)
            @block.scalar
            def _(s, d0=d0, d1=d1, t1_=t1_):
                s = ctx.S(s)
                ctx.v.wait(s, t1_)
                s.activation(tbuf[d0 % NSL][:], ubuf[d0 % NSL][:], AF.Tanh,
                             scale=0.7978845608028654)
                s.activation(tbuf[d1 % NSL][:], ubuf[d1 % NSL][:], AF.Tanh,
                             scale=0.7978845608028654)
            for k in (d0 - W2LAG, d1 - W2LAG):
                if k >= 0:
                    emit_w2(k)
            emitB_tail(d0)
            emitB_tail(d1)
        for dffc in range(32 - W2LAG, 32):
            emit_w2(dffc)
        nT_free[blk] = ctx.t.tok()
        @block.vector
        def _(v, blk=blk):
            v = ctx.V(v)
            ctx.t.wait(v)
            if blk >= 2:
                yst[blk % 2].wait(v)
            for tt in range(2):
                dst = ybuf[blk % 2] if tt == 0 else obuf[blk % 2]
                for ch in range(2):
                    k = tt * 2 + ch
                    v.tensor_add(dst[:, ch * 512:(ch + 1) * 512],
                                 ops[k][:], b2bc[:, ch * 512:(ch + 1) * 512])
                    ops_ev[k] = ctx.v.tok()
                ti = blk * 2 + tt
                v.bn_stats(stats[:, ti * 12:ti * 12 + 6], dst[:, 0:512])
                v.bn_stats(stats[:, ti * 12 + 6:ti * 12 + 12], dst[:, 512:1024])
        ev_toks.append(ctx.v.tok())
        @block.gpsimd
        def _(g, blk=blk, r0=r0):
            ctx.v.wait(g, ev_toks[blk])
            yst[blk % 2].inc(g.dma_start(out=scr_y[r0:r0 + 128, :], in_=ybuf[blk % 2][:]), 16)
            yst[blk % 2].inc(g.dma_start(out=scr_y[r0 + 128:r0 + 256, :], in_=obuf[blk % 2][:]), 16)

    if do_epi:
        @block.sync
        def _(q):
            for p in range(2):
                yst[p].wait(q)
        return emit_ln_residual_epilogue(ctx, scr_y, x_hbm, out_hbm, mean, var,
                                         rstd, xbuf, ybuf, obuf, stats,
                                         stats_ready=True, out_stats=out_stats)
    return None




S160 = 1.0 / np.sqrt(160.0)


def emit_phaseA(ctx, x_hbm, W, out_hbm, scr_n, scr_y, do_epi=True, nheads=8,
                in_stats=None, out_stats=None):
    nc, block = ctx.nc, ctx.block
    xbuf = [ctx.sbuf("A_x0", [128, 1024], F32), ctx.sbuf("A_x1", [128, 1024], F32)]
    nbuf = [ctx.sbuf("A_n0", [128, 1024], BF16), ctx.sbuf("A_n1", [128, 1024], BF16)]
    ybuf = [ctx.sbuf("A_y0", [128, 1024], F32), ctx.sbuf("A_y1", [128, 1024], F32)]
    obuf = [ctx.sbuf("A_o0", [128, 1024], F32), ctx.sbuf("A_o1", [128, 1024], F32)]
    stats = ctx.sbuf("A_stats", [128, NTILE * 12 + 2 * NTILE], F32)
    mean = ctx.sbuf("A_mean", [128, NTILE], F32)
    var = ctx.sbuf("A_var", [128, NTILE], F32)
    rstd = ctx.sbuf("A_rstd", [128, NTILE], F32)
    identf = ctx.sbuf("A_idf", [128, 128], F32)
    onesb = ctx.sbuf("A_ones", [128, 128], BF16)
    onesf1 = ctx.sbuf("A_onesf1", [1, 128], F32)
    nT = [ctx.sbuf("A_nT0", [128, 8 * 512], BF16), ctx.sbuf("A_nT1", [128, 8 * 512], BF16)]
    cqT = ctx.sbuf("A_cqT", [32, 4096], BF16)
    ckvT = ctx.sbuf("A_ckvT", [32, 4096], BF16)
    krT = ctx.sbuf("A_krT", [32, 4096], BF16)
    QcT = ctx.sbuf("A_QcT", [128, 2048], BF16)
    QrT = ctx.sbuf("A_QrT", [32, 2048], BF16)
    KcT = ctx.sbuf("A_KcT", [128, 2048], BF16)
    Vsb = ctx.sbuf("A_V", [128, 16 * 128], BF16)
    q2 = ctx.sbuf("A_q2", [128, 2048], BF16)
    q2r = ctx.sbuf("A_q2r", [32, 2048], BF16)
    nrm = ctx.sbuf("A_nrm", [1, 520], F32)
    ebias = ctx.sbuf("A_ebias", [128, 1], F32)
    expb = [ctx.sbuf("A_e%d" % k, [128, 512], BF16) for k in range(4)]
    drec = ctx.sbuf("A_dr", [128, 512], F32)
    oallT = ctx.sbuf("A_oallT", [128, 8 * 2048], BF16)
    wdqb = ctx.sbuf("A_wdq", [128, 8 * 32], BF16)
    wdkvb = ctx.sbuf("A_wdkv", [128, 8 * 32], BF16)
    wkrb = ctx.sbuf("A_wkr", [128, 8 * 32], BF16)
    wuqb = ctx.sbuf("A_wuq", [32, 1024], BF16)
    wukb = ctx.sbuf("A_wuk", [32, 1024], BF16)
    wuvb = ctx.sbuf("A_wuv", [32, 1024], BF16)
    wqrb = ctx.sbuf("A_wqr", [32, 256], BF16)
    wob = ctx.sbuf("A_wo", [128, 8 * 1024], BF16)
    b_dq = ctx.sbuf("A_bdq", [32, 1], F32)
    b_dkv = ctx.sbuf("A_bdkv", [32, 1], F32)
    b_kr = ctx.sbuf("A_bkr", [32, 1], F32)
    buq8 = ctx.sbuf("A_buq8", [8, 128], F32)
    buk8 = ctx.sbuf("A_buk8", [8, 128], F32)
    bqr8 = ctx.sbuf("A_bqr8", [8, 32], F32)
    buq_s = ctx.sbuf("A_buqs", [128, 8], F32)
    buk_c = ctx.sbuf("A_bukc", [128, 8], F32)
    bqr_s = ctx.sbuf("A_bqrs", [32, 8], F32)
    b_uvbc = ctx.sbuf("A_buvbc", [128, 1024], F32)
    b_obc = ctx.sbuf("A_bobc", [128, 1024], F32)
    qk_ps = [ctx.psum("A_qk0"),
             ctx.psum("A_qk1")]
    sc_ps = [ctx.psum("A_sc0"),
             ctx.psum("A_sc1")]
    den_ps = [ctx.psum("A_dn0"),
              ctx.psum("A_dn1")]
    oT_ps = [ctx.psum("A_oT0"),
             ctx.psum("A_oT1")]

    wsem = ctx.dsem("Aw", swdge=True)
    csem = ctx.dsem("Ac")
    @block.gpsimd
    def _(g):
        wsem.inc(g.dma_start(out=wdqb[:].rearrange("p (dc m) -> p dc m", dc=8),
                             in_=W["W_DQ"].rearrange("(dc p) m -> p dc m", p=128)), 16)
        wsem.inc(g.dma_start(out=wdkvb[:].rearrange("p (dc m) -> p dc m", dc=8),
                             in_=W["W_DKV"].rearrange("(dc p) m -> p dc m", p=128)), 16)
        wsem.inc(g.dma_start(out=wkrb[:].rearrange("p (dc m) -> p dc m", dc=8),
                             in_=W["W_KR"].rearrange("(dc p) m -> p dc m", p=128)), 16)
        wsem.inc(g.dma_start(out=wuqb[:], in_=W["W_UQ"][:, :]), 16)
        wsem.inc(g.dma_start(out=wukb[:], in_=W["W_UK"][:, :]), 16)
        wsem.inc(g.dma_start(out=wuvb[:], in_=W["W_UV"][:, :]), 16)
        wsem.inc(g.dma_start(out=wqrb[:], in_=W["W_QR"][:, :]), 16)
        wsem.inc(g.dma_start(out=wob[:].rearrange("p (dc c) -> p dc c", dc=8),
                             in_=W["W_O"].rearrange("(dc p) c -> p dc c", p=128)), 16)
    @block.sync
    def _(q):
        csem.inc(q.dma_start(out=identf[:], in_=ctx.ident_hbm[:, :]), 16)
        csem.inc(q.dma_start(out=onesb[:], in_=ctx.ones_hbm[:, :]), 16)
        csem.inc(q.dma_start(out=onesf1[:], in_=ctx.onesf_hbm[:, :]), 16)
        csem.inc(q.dma_start(out=b_dq[:], in_=W["b_DQ"][:, None]), 16)
        csem.inc(q.dma_start(out=b_dkv[:], in_=W["b_DKV"][:, None]), 16)
        csem.inc(q.dma_start(out=b_kr[:], in_=W["b_KR"][:, None]), 16)
        csem.inc(q.dma_start(out=buq8[:], in_=W["b_UQ"].rearrange("(h c) -> h c", c=128)), 16)
        csem.inc(q.dma_start(out=buk8[:], in_=W["b_UK"].rearrange("(h c) -> h c", c=128)), 16)
        csem.inc(q.dma_start(out=bqr8[:], in_=W["b_QR"].rearrange("(h c) -> h c", c=32)), 16)
        csem.inc(q.dma_start(out=b_uvbc[:], in_=W["b_UV"][None, :].partition_broadcast(128)), 16)
        csem.inc(q.dma_start(out=b_obc[:], in_=W["b_O"][None, :].partition_broadcast(128)), 16)
    @block.tensor
    def _(te):
        csem.wait(te)
        te.transpose(qk_ps[0][:, 0:8], buq8[:, :], identf[0:8, 0:8])
        te.transpose(qk_ps[1][:, 0:8], buk8[:, :], identf[0:8, 0:8])
        ctx.t.inc(te.transpose(sc_ps[0][0:32, 0:8], bqr8[:, :], identf[0:8, 0:8]))
    @block.vector
    def _(v):
        v = ctx.V(v)
        ctx.t.wait(v)
        v.tensor_scalar_mul(buq_s[:], qk_ps[0][:, 0:8], S160)
        v.tensor_copy(buk_c[:], qk_ps[1][:, 0:8])
        v.tensor_scalar_mul(bqr_s[:], sc_ps[0][0:32, 0:8], S160)
    bias_tok = ctx.v.tok()

    nTs = [ctx.dsem("AnT"), ctx.dsem("AnT")]
    nT_done = [None, None]
    ps_free = [None, None]

    def _A_load_chunk(tc, stsems):
        pn = tc % 2
        @block.sync
        def _(q, tc=tc, pn=pn, stsems=stsems):
            for p2 in range(2):
                stsems[p2].wait(q)
            if tc >= 2:
                ctx.t.wait(q, nT_done[pn])
            for dc in range(8):
                nTs[pn].inc(q.dma_start(out=nT[pn][:, dc * 512:(dc + 1) * 512],
                                        in_=scr_n[tc * 512:(tc + 1) * 512,
                                                  dc * 128:(dc + 1) * 128],
                                        transpose=True), 16)

    n_st = emit_ln_pass(ctx, x_hbm, scr_n, mean, var, rstd, xbuf, nbuf, stats,
                        chunk_cb=_A_load_chunk, in_stats=in_stats)

    for tc in range(8):
        pn = tc % 2
        if tc >= 2:
            _A_load_chunk(tc, n_st)
        for j, (wb_, bb_, outT_) in enumerate([(wdqb, b_dq, cqT), (wdkvb, b_dkv, ckvT),
                                               (wkrb, b_kr, krT)]):
            p = (tc * 3 + j) % 2
            @block.tensor
            def _(te, j=j, tc=tc, p=p, pn=pn, wb=wb_):
                if j == 0 and tc == 0:
                    wsem.wait(te)
                if j == 0:
                    nTs[pn].wait(te)
                gwait(te, ps_free[p])
                for dc in range(8):
                    ins = te.matmul(qk_ps[p][0:32, :],
                                    wb[:, dc * 32:(dc + 1) * 32],
                                    nT[pn][:, dc * 512:(dc + 1) * 512],
                                    start=(dc == 0), stop=(dc == 7))
                ctx.t.inc(ins)
            mm_tok = ctx.t.tok()
            if j == 2:
                nT_done[pn] = ctx.t.tok()
            @block.vector
            def _(v, tc=tc, p=p, bb=bb_, outT=outT_, mm_tok=mm_tok):
                v = ctx.V(v)
                ctx.t.wait(v, mm_tok)
                v.tensor_scalar(outT[:, tc * 512:(tc + 1) * 512], qk_ps[p][0:32, :],
                                bb[0:32, 0:1], None, ALU.add)
            ps_free[p] = (ctx.v, ctx.v.tok())

    yst = [ctx.gst[2], ctx.gst[3]]
    ev_toks_A = []
    stage1_free = list(ps_free)
    scb = [sc_ps[0], sc_ps[1], qk_ps[0], qk_ps[1]]
    sc_free = [0, 0, 0, 0]
    expb_free = [0, 0, 0, 0]
    oT_free = [0, 0]
    den_free = [0, 0]
    for b in range(B_PER_CORE):
        for h in range(nheads):
            t0 = b * 2048
            for role in range(3):
                for tc in range(4):
                    p = (role * 4 + tc) % 2
                    @block.tensor
                    def _(te, role=role, tc=tc, p=p, h=h, t0=t0):
                        gwait(te, stage1_free[p])
                        ctx.s.wait(te, sc_free[2 + p])
                        ctx.v.wait(te, bias_tok)
                        if role == 0:
                            ins = te.matmul(qk_ps[p][:], wuqb[:, h * 128:(h + 1) * 128],
                                            cqT[:, t0 + tc * 512: t0 + (tc + 1) * 512],
                                            start=True, stop=True)
                        elif role == 1:
                            ins = te.matmul(qk_ps[p][:], wukb[:, h * 128:(h + 1) * 128],
                                            ckvT[:, t0 + tc * 512: t0 + (tc + 1) * 512],
                                            start=True, stop=True)
                        else:
                            ins = te.matmul(qk_ps[p][0:32, :], wqrb[:, h * 32:(h + 1) * 32],
                                            cqT[:, t0 + tc * 512: t0 + (tc + 1) * 512],
                                            start=True, stop=True)
                        ctx.t.inc(ins)
                    mm_tok = ctx.t.tok()
                    @block.vector
                    def _(v, role=role, tc=tc, p=p, h=h, mm_tok=mm_tok):
                        v = ctx.V(v)
                        ctx.t.wait(v, mm_tok)
                        if role == 0:
                            v.tensor_scalar(QcT[:, tc * 512:(tc + 1) * 512], qk_ps[p][:],
                                            S160, buq_s[:, h:h + 1], ALU.mult, ALU.add)
                        elif role == 1:
                            v.tensor_scalar(KcT[:, tc * 512:(tc + 1) * 512], qk_ps[p][:],
                                            buk_c[:, h:h + 1], None, ALU.add)
                        else:
                            v.tensor_scalar(QrT[:, tc * 512:(tc + 1) * 512], qk_ps[p][0:32, :],
                                            S160, bqr_s[0:32, h:h + 1], ALU.mult, ALU.add)
                    stage1_free[p] = (ctx.v, ctx.v.tok())
            v_toks = []
            for tt in range(16):
                p = tt % 2
                @block.tensor
                def _(te, tt=tt, p=p, h=h, t0=t0):
                    if tt < 2:
                        gwait(te, stage1_free[p])
                    if len(v_toks) >= 2:
                        ctx.v.wait(te, v_toks[-2])
                    ins = te.matmul(qk_ps[p][:, 0:128],
                                    ckvT[:, t0 + tt * 128: t0 + (tt + 1) * 128],
                                    wuvb[:, h * 128:(h + 1) * 128],
                                    start=True, stop=True)
                    ctx.t.inc(ins)
                mm_tok = ctx.t.tok()
                @block.vector
                def _(v, tt=tt, p=p, h=h, mm_tok=mm_tok):
                    v = ctx.V(v)
                    ctx.t.wait(v, mm_tok)
                    v.tensor_add(Vsb[:, tt * 128:(tt + 1) * 128], qk_ps[p][:, 0:128],
                                 b_uvbc[:, h * 128:(h + 1) * 128])
                    v_toks.append(ctx.v.tok())
            stage1_free = [(ctx.v, v_toks[-2]), (ctx.v, v_toks[-1])]
            @block.vector
            def _(v):
                v = ctx.V(v)
                v.tensor_mul(q2[:], QcT[:], QcT[:])
                v.tensor_mul(q2r[:], QrT[:], QrT[:])
            q2_tok = ctx.v.tok()
            @block.tensor
            def _(te, q2_tok=q2_tok):
                ctx.v.wait(te, q2_tok)
                ctx.v.wait(te, den_free[0])
                for c in range(4):
                    te.matmul(den_ps[0][0:1, :], onesb[:, 0:1],
                              q2[:, c * 512:(c + 1) * 512], start=(c == 0), stop=False)
                    ins = te.matmul(den_ps[0][0:1, :], onesb[0:32, 0:1],
                                    q2r[:, c * 512:(c + 1) * 512], start=False,
                                    stop=(c == 3))
                ctx.t.inc(ins)
            nq_tok = ctx.t.tok()
            @block.vector
            def _(v, nq_tok=nq_tok):
                v = ctx.V(v)
                ctx.t.wait(v, nq_tok)
                v.reduce_max(nrm[0:1, 512:513], den_ps[0][0:1, :], axis=AX.X)
            den_free[0] = ctx.v.tok()
            @block.vector
            def _(v):
                v = ctx.V(v)
                v.tensor_mul(q2[:], KcT[:], KcT[:])
                kr = krT[:, t0:t0 + 2048]
                v.tensor_mul(q2r[:], kr, kr)
            k2_tok = ctx.v.tok()
            @block.tensor
            def _(te, k2_tok=k2_tok):
                ctx.v.wait(te, k2_tok)
                ctx.v.wait(te, den_free[0])
                for c in range(4):
                    te.matmul(den_ps[0][0:1, :], onesb[:, 0:1],
                              q2[:, c * 512:(c + 1) * 512], start=(c == 0), stop=False)
                    ins = te.matmul(den_ps[0][0:1, :], onesb[0:32, 0:1],
                                    q2r[:, c * 512:(c + 1) * 512], start=False,
                                    stop=(c == 3))
                ctx.t.inc(ins)
            nk_tok = ctx.t.tok()
            @block.vector
            def _(v, nk_tok=nk_tok):
                v = ctx.V(v)
                ctx.t.wait(v, nk_tok)
                v.reduce_max(nrm[0:1, 513:514], den_ps[0][0:1, :], axis=AX.X)
                v.tensor_mul(nrm[0:1, 514:515], nrm[0:1, 512:513], nrm[0:1, 513:514])
                v.tensor_scalar_max(nrm[0:1, 514:515], nrm[0:1, 514:515], 1e-30)
            den_free[0] = ctx.v.tok()
            prod_tok = ctx.v.tok()
            @block.scalar
            def _(s, prod_tok=prod_tok):
                s = ctx.S(s)
                ctx.v.wait(s, prod_tok)
                s.activation(nrm[0:1, 515:516], nrm[0:1, 514:515], AF.Ln)
                s.activation(nrm[0:1, 516:517], nrm[0:1, 515:516], AF.Exp, scale=0.5)
            B_tok = ctx.s.tok()
            @block.vector
            def _(v, B_tok=B_tok):
                v = ctx.V(v)
                ctx.s.wait(v, B_tok)
                v.tensor_scalar_mul(nrm[0:1, 517:518], nrm[0:1, 516:517], -1.0)
            negB_tok = ctx.v.tok()
            @block.tensor
            def _(te, negB_tok=negB_tok):
                ctx.v.wait(te, negB_tok)
                ctx.v.wait(te, den_free[1])
                ctx.t.inc(te.matmul(den_ps[1][:, 0:1], onesf1[0:1, 0:128],
                                    nrm[0:1, 517:518], start=True, stop=True))
            bb_tok = ctx.t.tok()
            @block.vector
            def _(v, bb_tok=bb_tok):
                v = ctx.V(v)
                ctx.t.wait(v, bb_tok)
                v.tensor_copy(ebias[:], den_ps[1][:, 0:1])
            den_free[1] = ctx.v.tok()
            ebias_tok = ctx.v.tok()
            for tc in range(4):
                po = tc % 2
                exp_toks = {}

                def emit_scmm(kt, tc=tc, po=po, t0=t0):
                    pk = kt % 4
                    @block.tensor
                    def _(te, tc=tc, kt=kt, pk=pk, po=po, t0=t0):
                        ctx.s.wait(te, sc_free[pk])
                        if tc == 0 and kt in (2, 3):
                            ctx.v.wait(te, ebias_tok)
                        if kt == 0:
                            ctx.v.wait(te, oT_free[po])
                            ctx.v.wait(te, den_free[po])
                        te.matmul(scb[pk][:], KcT[:, kt * 128:(kt + 1) * 128],
                                  QcT[:, tc * 512:(tc + 1) * 512], start=True, stop=False)
                        ctx.t.inc(te.matmul(scb[pk][:],
                                            krT[:, t0 + kt * 128: t0 + (kt + 1) * 128],
                                            QrT[:, tc * 512:(tc + 1) * 512],
                                            start=False, stop=True))
                    exp_toks[("mm", kt)] = ctx.t.tok()

                def emit_exp_pair(k0, k1):
                    @block.scalar
                    def _(s, k0=k0, k1=k1):
                        s = ctx.S(s)
                        ctx.t.wait(s, exp_toks[("mm", k1)])
                        ctx.v.wait(s, ebias_tok)
                        for kk in (k0, k1):
                            pk = kk % 4
                            ctx.t.wait(s, expb_free[pk])
                            s.activation(expb[pk][:], scb[pk][:], AF.Exp,
                                         bias=ebias[:, 0:1])
                            sc_free[pk] = ctx.s.tok()
                            exp_toks[kk] = ctx.s.tok()

                def emit_pv(kt, po=po):
                    pk = kt % 4
                    @block.tensor
                    def _(te, kt=kt, pk=pk, po=po):
                        ctx.s.wait(te, exp_toks[kt])
                        te.matmul(oT_ps[po][:], Vsb[:, kt * 128:(kt + 1) * 128],
                                  expb[pk][:], start=(kt == 0), stop=(kt == 15))
                        ins = te.matmul(den_ps[po][:], onesb[:, 0:128],
                                        expb[pk][:], start=(kt == 0), stop=(kt == 15))
                        ctx.t.inc(ins)
                    expb_free[pk] = ctx.t.tok()

                for kt in range(4):
                    emit_scmm(kt)
                emit_exp_pair(0, 1)
                for p2 in range(8):
                    k0 = 2 * p2
                    if k0 + 4 < 16:
                        emit_scmm(k0 + 4)
                    if k0 + 5 < 16:
                        emit_scmm(k0 + 5)
                    if k0 + 2 < 16:
                        emit_exp_pair(k0 + 2, k0 + 3)
                    emit_pv(k0)
                    emit_pv(k0 + 1)
                pv_tok = ctx.t.tok()
                @block.vector
                def _(v, tc=tc, po=po, h=h, pv_tok=pv_tok):
                    v = ctx.V(v)
                    ctx.t.wait(v, pv_tok)
                    v.reciprocal(drec[:], den_ps[po][:])
                    v.tensor_mul(oallT[:, h * 2048 + tc * 512: h * 2048 + (tc + 1) * 512],
                                 oT_ps[po][:], drec[:])
                den_free[po] = ctx.v.tok()
                oT_free[po] = ctx.v.tok()
        for tt in range(16):
            @block.tensor
            def _(te, tt=tt, b=b):
                ctx.v.wait(te)
                ctx.s.wait(te)
                if tt >= 2:
                    ctx.v.wait(te, ev_toks_A[-2])
                for ch in range(2):
                    for dall in range(8):
                        ins = te.matmul(qk_ps[ch][:],
                                        oallT[:, dall * 2048 + tt * 128: dall * 2048 + (tt + 1) * 128],
                                        wob[:, dall * 1024 + ch * 512: dall * 1024 + (ch + 1) * 512],
                                        start=(dall == 0), stop=(dall == 7))
                    ctx.t.inc(ins)
            mm_tok = ctx.t.tok()
            @block.vector
            def _(v, tt=tt, b=b, mm_tok=mm_tok):
                v = ctx.V(v)
                ctx.t.wait(v, mm_tok)
                yst[tt % 2].wait(v)
                dst = ybuf[tt % 2]
                for ch in range(2):
                    v.tensor_add(dst[:, ch * 512:(ch + 1) * 512], qk_ps[ch][:],
                                 b_obc[:, ch * 512:(ch + 1) * 512])
                ti = b * 16 + tt
                v.bn_stats(stats[:, ti * 12:ti * 12 + 6], dst[:, 0:512])
                v.bn_stats(stats[:, ti * 12 + 6:ti * 12 + 12], dst[:, 512:1024])
                ev_toks_A.append(ctx.v.tok())
            @block.gpsimd
            def _(g, tt=tt, b=b, tok=None):
                ctx.v.wait(g, ev_toks_A[-1])
                r0 = b * 2048 + tt * 128
                yst[tt % 2].inc(g.dma_start(out=scr_y[r0:r0 + 128, :],
                                            in_=ybuf[tt % 2][:]), 16)
        stage1_free = [(ctx.v, ctx.v.tok()), (ctx.v, ctx.v.tok())]

    if do_epi:
        @block.sync
        def _(q):
            for p in range(2):
                yst[p].wait(q)
        return emit_ln_residual_epilogue(ctx, scr_y, x_hbm, out_hbm, mean, var,
                                         rstd, xbuf, ybuf, obuf, stats,
                                         stats_ready=True, out_stats=out_stats)
    return None




def emit_phaseC(ctx, x_hbm, W, out_hbm, scr_n, scr_y, do_epi=True,
                in_stats=None, out_stats=None):
    nc, block = ctx.nc, ctx.block
    xbuf = [ctx.sbuf("C_x0", [128, 1024], F32), ctx.sbuf("C_x1", [128, 1024], F32)]
    nbuf = [ctx.sbuf("C_n0", [128, 1024], BF16), ctx.sbuf("C_n1", [128, 1024], BF16)]
    ybuf = [ctx.sbuf("C_y0", [128, 1024], F32), ctx.sbuf("C_y1", [128, 1024], F32)]
    obuf = [ctx.sbuf("C_o0", [128, 1024], F32), ctx.sbuf("C_o1", [128, 1024], F32)]
    stats = ctx.sbuf("C_stats", [128, NTILE * 12 + 2 * NTILE], F32)
    mean = ctx.sbuf("C_mean", [128, NTILE], F32)
    var = ctx.sbuf("C_var", [128, NTILE], F32)
    rstd = ctx.sbuf("C_rstd", [128, NTILE], F32)
    identf = ctx.sbuf("C_idf", [128, 128], F32)
    onesf1 = ctx.sbuf("C_onesf1", [1, 128], F32)
    ccum = ctx.sbuf("C_ccum", [128, 128], F32)
    csuf = ctx.sbuf("C_csuf", [128, 128], F32)
    cmsl = ctx.sbuf("C_cmsl", [128, 128], F32)
    cmui = ctx.sbuf("C_cmui", [128, 128], F32)
    nT = [ctx.sbuf("C_nT0", [128, 8 * 512], BF16), ctx.sbuf("C_nT1", [128, 8 * 512], BF16)]
    QT = ctx.sbuf("C_QT", [128, 4096], F32)
    Ktm = ctx.sbuf("C_Ktm", [128, 4096], F32)
    Vt = ctx.sbuf("C_Vt", [128, 4096], F32)
    Gt = ctx.sbuf("C_Gt", [128, 4096], F32)
    la_sb = ctx.sbuf("C_la", [128, 32], F32)
    beta_sb = ctx.sbuf("C_beta", [128, 32], F32)
    oallTC = ctx.sbuf("C_oallT", [128, 4096], BF16)
    tmpk = ctx.sbuf("C_tmpk", [128, 128], F32)
    tmpg = ctx.sbuf("C_tmpg", [128, 128], F32)
    tmp2 = ctx.sbuf("C_tmp2", [128, 128], F32)
    sscol = ctx.sbuf("C_ss", [128, 4], F32)
    tmpab = ctx.sbuf("C_tmpab", [128, 2], F32)
    wqb = ctx.sbuf("C_wq", [128, 8 * 128], BF16)
    wkb = ctx.sbuf("C_wk", [128, 8 * 128], BF16)
    wvb = ctx.sbuf("C_wv", [128, 8 * 128], BF16)
    wgb = ctx.sbuf("C_wg", [128, 8 * 128], BF16)
    wabb = ctx.sbuf("C_wab", [128, 8 * 2], BF16)
    wob = ctx.sbuf("C_wo", [128, 1024], BF16)
    bq_c = ctx.sbuf("C_bq", [128, 1], F32)
    bk_bc = ctx.sbuf("C_bkbc", [128, 128], F32)
    bv_bc = ctx.sbuf("C_bvbc", [128, 128], F32)
    bg_bc = ctx.sbuf("C_bgbc", [128, 128], F32)
    bab_bc = ctx.sbuf("C_babbc", [128, 2], F32)
    bo_bc = ctx.sbuf("C_bobc", [128, 1024], F32)
    pb = []
    for b in range(2):
        d = {}
        d["lgch"] = [ctx.sbuf("C_lg%d_%d" % (b, i), [128, 8], F32) for i in range(4)]
        d["lgrow"] = ctx.sbuf("C_lgr1_%d" % b, [1, 128], F32)
        d["lgr"] = ctx.sbuf("C_lgr%d" % b, [128, 128], F32)
        d["dbuf"] = ctx.sbuf("C_db%d" % b, [128, 128], F32)
        d["dl"] = ctx.sbuf("C_dl%d" % b, [128, 128], F32)
        d["du"] = ctx.sbuf("C_du%d" % b, [128, 128], F32)
        d["ktT"] = [ctx.sbuf("C_ktT%d_%d" % (b, i), [128, 128], F32) for i in range(4)]
        d["Nb"] = [ctx.sbuf("C_N%d_%d" % (b, i), [128, 128], F32) for i in range(2)]
        d["Tb"] = [ctx.sbuf("C_T%d_%d" % (b, i), [128, 128], F32) for i in range(2)]
        d["U2"] = [ctx.sbuf("C_U2%d_%d" % (b, i), [128, 256], F32) for i in range(4)]
        d["WrT"] = [ctx.sbuf("C_WrT%d_%d" % (b, i), [128, 128], F32) for i in range(4)]
        d["Gv"] = [ctx.sbuf("C_Gv%d_%d" % (b, i), [128, 128], F32) for i in range(4)]
        d["maskD"] = [ctx.sbuf("C_mD%d_%d" % (b, i), [128, 128], F32) for i in range(4)]
        d["Ub"] = ctx.sbuf("C_U%d" % b, [128, 128], F32)
        d["Us"] = ctx.sbuf("C_Us%d" % b, [128, 128], F32)
        d["rq"] = ctx.sbuf("C_rq%d" % b, [128, 128], F32)
        d["WtT"] = ctx.sbuf("C_Wt%d" % b, [128, 128], F32)
        d["t1"] = ctx.sbuf("C_t1%d" % b, [128, 128], F32)
        d["t2"] = ctx.sbuf("C_t2%d" % b, [128, 128], F32)
        d["bg"] = ctx.sbuf("C_bg%d" % b, [128, 4], F32)
        d["OC"] = ctx.sbuf("C_OC%d" % b, [128, 128], F32)
        d["R"] = [ctx.sbuf("C_R%d_%d" % (b, i), [128, 128], F32) for i in range(4)]
        d["RT"] = [ctx.sbuf("C_RT%d_%d" % (b, i), [128, 128], F32) for i in range(4)]
        pb.append(d)
    pj_ps = [ctx.psum("C_pj0"),
             ctx.psum("C_pj1")]
    ck_ps = [[ctx.psum("C_ck%d_%d" % (b, i))
              for i in range(2)] for b in range(2)]
    out_ps = [ctx.psum("C_op0"),
              ctx.psum("C_op1")]

    wsem = ctx.dsem("Cw", swdge=True)
    csem = ctx.dsem("Cc")
    @block.gpsimd
    def _(g):
        for nm, buf, mm in [("Wq", wqb, 128), ("Wk", wkb, 128), ("Wv", wvb, 128),
                            ("Wg", wgb, 128), ("Wab", wabb, 2)]:
            wsem.inc(g.dma_start(out=buf[:].rearrange("p (dc m) -> p dc m", dc=8),
                                 in_=W[nm].rearrange("(dc p) m -> p dc m", p=128)), 16)
        wsem.inc(g.dma_start(out=wob[:], in_=W["Wo"][:, :]), 16)
    @block.sync
    def _(q):
        csem.inc(q.dma_start(out=identf[:], in_=ctx.ident_hbm[:, :]), 16)
        csem.inc(q.dma_start(out=onesf1[:], in_=ctx.onesf_hbm[:, :]), 16)
        csem.inc(q.dma_start(out=ccum[:], in_=ctx.ccum_hbm[:, :]), 16)
        csem.inc(q.dma_start(out=csuf[:], in_=ctx.csuf_hbm[:, :]), 16)
        csem.inc(q.dma_start(out=cmsl[:], in_=ctx.cmsl_hbm[:, :]), 16)
        csem.inc(q.dma_start(out=cmui[:], in_=ctx.cmui_hbm[:, :]), 16)
        csem.inc(q.dma_start(out=bq_c[:], in_=W["bq"][:, None]), 16)
        csem.inc(q.dma_start(out=bk_bc[:], in_=W["bk"][None, :].partition_broadcast(128)), 16)
        csem.inc(q.dma_start(out=bv_bc[:], in_=W["bv"][None, :].partition_broadcast(128)), 16)
        csem.inc(q.dma_start(out=bg_bc[:], in_=W["bg"][None, :].partition_broadcast(128)), 16)
        csem.inc(q.dma_start(out=bab_bc[:], in_=W["bab"][None, :].partition_broadcast(128)), 16)
        csem.inc(q.dma_start(out=bo_bc[:], in_=W["bo"][None, :].partition_broadcast(128)), 16)

    nTs = [ctx.dsem("CnT"), ctx.dsem("CnT")]
    nT_done = [None, None]
    pj_free = [None, None]

    def _C_load_chunk(tc, stsems):
        pn = tc % 2
        @block.sync
        def _(q, tc=tc, pn=pn, stsems=stsems):
            for p2 in range(2):
                stsems[p2].wait(q)
            if tc >= 2:
                ctx.t.wait(q, nT_done[pn])
            for dc in range(8):
                nTs[pn].inc(q.dma_start(out=nT[pn][:, dc * 512:(dc + 1) * 512],
                                        in_=scr_n[tc * 512:(tc + 1) * 512,
                                                  dc * 128:(dc + 1) * 128],
                                        transpose=True), 16)

    n_st = emit_ln_pass(ctx, x_hbm, scr_n, mean, var, rstd, xbuf, nbuf, stats,
                        chunk_cb=_C_load_chunk, in_stats=in_stats)

    for tc in range(8):
        pn = tc % 2
        if tc >= 2:
            _C_load_chunk(tc, n_st)
        @block.tensor
        def _(te, tc=tc, pn=pn):
            if tc == 0:
                wsem.wait(te)
                csem.wait(te)
            nTs[pn].wait(te)
            gwait(te, pj_free[0])
            for dc in range(8):
                ins = te.matmul(pj_ps[0][:], wqb[:, dc * 128:(dc + 1) * 128],
                                nT[pn][:, dc * 512:(dc + 1) * 512],
                                start=(dc == 0), stop=(dc == 7))
            ctx.t.inc(ins)
        mm_tok = ctx.t.tok()
        @block.vector
        def _(v, tc=tc, mm_tok=mm_tok):
            v = ctx.V(v)
            ctx.t.wait(v, mm_tok)
            v.tensor_scalar(QT[:, tc * 512:(tc + 1) * 512], pj_ps[0][:],
                            bq_c[:, 0:1], None, ALU.add)
        pj_free[0] = (ctx.v, ctx.v.tok())
        for i in range(4):
            tg = tc * 4 + i
            for role in range(4):
                @block.tensor
                def _(te, pn=pn, i=i, role=role):
                    gwait(te, pj_free[1])
                    wb2 = [wkb, wvb, wgb, wabb][role]
                    nw = 128 if role < 3 else 2
                    for dc in range(8):
                        ins = te.matmul(pj_ps[1][:, 0:nw],
                                        nT[pn][:, dc * 512 + i * 128: dc * 512 + (i + 1) * 128],
                                        wb2[:, dc * nw:(dc + 1) * nw],
                                        start=(dc == 0), stop=(dc == 7))
                    ctx.t.inc(ins)
                mm_tok = ctx.t.tok()
                if role == 0:
                    @block.vector
                    def _(v, mm_tok=mm_tok):
                        v = ctx.V(v)
                        ctx.t.wait(v, mm_tok)
                        v.tensor_add(tmpk[:], pj_ps[1][:, 0:128], bk_bc[:])
                        v.tensor_mul(tmp2[:], tmpk[:], tmpk[:])
                        v.reduce_sum(sscol[:, 0:1], tmp2[:], axis=AX.X)
                    ss_tok = ctx.v.tok()
                    @block.scalar
                    def _(s, ss_tok=ss_tok):
                        s = ctx.S(s)
                        ctx.v.wait(s, ss_tok)
                        s.activation(sscol[:, 1:2], sscol[:, 0:1], AF.Ln)
                        s.activation(sscol[:, 1:2], sscol[:, 1:2], AF.Exp, scale=-0.5)
                    rs_tok = ctx.s.tok()
                    @block.vector
                    def _(v, tg=tg, rs_tok=rs_tok):
                        v = ctx.V(v)
                        ctx.s.wait(v, rs_tok)
                        v.tensor_scalar_min(sscol[:, 1:2], sscol[:, 1:2], 1e12)
                        v.tensor_scalar_mul(Ktm[:, tg * 128:(tg + 1) * 128], tmpk[:],
                                            sscol[:, 1:2])
                    pj_free[1] = (ctx.v, ctx.v.tok())
                elif role == 1:
                    @block.vector
                    def _(v, tg=tg, mm_tok=mm_tok):
                        v = ctx.V(v)
                        ctx.t.wait(v, mm_tok)
                        v.tensor_add(Vt[:, tg * 128:(tg + 1) * 128], pj_ps[1][:, 0:128],
                                     bv_bc[:])
                    pj_free[1] = (ctx.v, ctx.v.tok())
                elif role == 2:
                    @block.vector
                    def _(v, mm_tok=mm_tok):
                        v = ctx.V(v)
                        ctx.t.wait(v, mm_tok)
                        v.tensor_add(tmpg[:], pj_ps[1][:, 0:128], bg_bc[:])
                    zg_tok = ctx.v.tok()
                    pj_free[1] = (ctx.v, ctx.v.tok())
                    @block.scalar
                    def _(s, zg_tok=zg_tok):
                        s = ctx.S(s)
                        ctx.v.wait(s, zg_tok)
                        s.activation(tmpg[:], tmpg[:], AF.Exp, scale=-1.0)
                    eg_tok = ctx.s.tok()
                    @block.vector
                    def _(v, tg=tg, eg_tok=eg_tok):
                        v = ctx.V(v)
                        ctx.s.wait(v, eg_tok)
                        v.tensor_scalar_add(tmpg[:], tmpg[:], 1.0)
                        v.reciprocal(Gt[:, tg * 128:(tg + 1) * 128], tmpg[:])
                else:
                    @block.vector
                    def _(v, mm_tok=mm_tok):
                        v = ctx.V(v)
                        ctx.t.wait(v, mm_tok)
                        v.tensor_add(tmpab[:], pj_ps[1][:, 0:2], bab_bc[:])
                    zab_tok = ctx.v.tok()
                    pj_free[1] = (ctx.v, ctx.v.tok())
                    @block.scalar
                    def _(s, zab_tok=zab_tok):
                        s = ctx.S(s)
                        ctx.v.wait(s, zab_tok)
                        s.activation(tmpab[:], tmpab[:], AF.Exp, scale=-1.0)
                    eab_tok = ctx.s.tok()
                    @block.vector
                    def _(v, tg=tg, eab_tok=eab_tok):
                        v = ctx.V(v)
                        ctx.s.wait(v, eab_tok)
                        v.tensor_scalar_add(tmpab[:], tmpab[:], 1.0)
                        v.reciprocal(beta_sb[:, tg:tg + 1], tmpab[:, 1:2])
                    b2_tok = ctx.v.tok()
                    @block.scalar
                    def _(s, b2_tok=b2_tok):
                        s = ctx.S(s)
                        ctx.v.wait(s, b2_tok)
                        s.activation(sscol[:, 2:3], tmpab[:, 0:1], AF.Ln)
                    lnt_tok = ctx.s.tok()
                    @block.vector
                    def _(v, tg=tg, lnt_tok=lnt_tok):
                        v = ctx.V(v)
                        ctx.s.wait(v, lnt_tok)
                        v.tensor_scalar_mul(la_sb[:, tg:tg + 1], sscol[:, 2:3], -1.0)

    ck_free = [[None, None], [None, None]]
    rch_free = [None, None]
    out_free = [None, None]
    cur = [0, 0]
    NRS = 4

    def te_step(bx, bank, fn):
        @block.tensor
        def _(te, bx=bx, bank=bank, fn=fn):
            gwait(te, ck_free[bx][bank])
            fn(te, ck_ps[bx][bank])
        return ctx.t.tok()

    @block.vector
    def _(v):
        v = ctx.V(v)
        for b in range(2):
            v.memset(pb[b]["R"][0][:], 0.0)
            v.memset(pb[b]["RT"][0][:], 0.0)
    rinit_tok = ctx.v.tok()

    offready = [[None] * 16, [None] * 16]
    onready = [[None] * 16, [None] * 16]
    ucons = [[None] * 16, [None] * 16]
    rsnap = [[0] * 16, [0] * 16]

    def emit_off(ch, b):
        d = pb[b]
        sl = ch % 4
        tg = b * 16 + ch
        lach = la_sb[:, tg:tg + 1]
        bech = beta_sb[:, tg:tg + 1]
        ktm_c = Ktm[:, tg * 128:(tg + 1) * 128]
        v_c = Vt[:, tg * 128:(tg + 1) * 128]
        qT_c = QT[:, tg * 128:(tg + 1) * 128]
        lgch = d["lgch"][sl]
        def f_lg(te, ps, lach=lach, ch=ch, b=b):
            if ch == 0 and b == 0:
                ctx.v.wait(te)
                ctx.s.wait(te)
                ctx.v.wait(te, rinit_tok)
            te.matmul(ps[:, 0:1], ccum[:], lach, start=True, stop=True)
            ctx.t.inc(te.matmul(ps[:, 1:2], csuf[:], lach, start=True, stop=True))
        te_step(b, 0, f_lg)
        def f_kt(te, ps, ktm_c=ktm_c):
            ctx.t.inc(te.transpose(ps[:, 0:128], ktm_c, identf[:]))
        t_kt = te_step(b, 1, f_kt)
        @block.vector
        def _(v, d=d, b=b, sl=sl, lgch=lgch, t_kt=t_kt):
            v = ctx.V(v)
            ctx.t.wait(v, t_kt)
            v.tensor_copy(lgch[:, 0:2], ck_ps[b][0][:, 0:2])
            v.tensor_copy(d["ktT"][sl][:], ck_ps[b][1][:, 0:128])
            v.tensor_add(lgch[:, 5:6], lgch[:, 0:1], lgch[:, 1:2])
        lg_tok = ctx.v.tok()
        ck_free[b][0] = (ctx.v, lg_tok)
        ck_free[b][1] = (ctx.v, lg_tok)
        @block.scalar
        def _(s, lgch=lgch, lg_tok=lg_tok):
            s = ctx.S(s)
            ctx.v.wait(s, lg_tok)
            s.activation(lgch[:, 2:3], lgch[:, 0:1], AF.Exp)
            s.activation(lgch[:, 3:4], lgch[:, 1:2], AF.Exp)
            s.activation(lgch[:, 4:5], lgch[:, 5:6], AF.Exp)
        gam_tok = ctx.s.tok()
        def f_lt(te, ps, d=d, lgch=lgch, lg_tok=lg_tok):
            ctx.v.wait(te, lg_tok)
            ctx.t.inc(te.transpose(ps[0:1, 0:128], lgch[:, 0:1], identf[:]))
        t_lt = te_step(b, 0, f_lt)
        @block.vector
        def _(v, d=d, b=b, t_lt=t_lt):
            v = ctx.V(v)
            ctx.t.wait(v, t_lt)
            v.tensor_copy(d["lgrow"][:], ck_ps[b][0][0:1, 0:128])
        lr_tok = ctx.v.tok()
        ck_free[b][0] = (ctx.v, lr_tok)
        def f_lb(te, ps, d=d, lr_tok=lr_tok):
            ctx.v.wait(te, lr_tok)
            ctx.t.inc(te.matmul(ps[:, 0:128], onesf1[0:1, :], d["lgrow"][:],
                                start=True, stop=True))
        t_lb = te_step(b, 0, f_lb)
        @block.vector
        def _(v, d=d, b=b, lgch=lgch, t_lb=t_lb):
            v = ctx.V(v)
            ctx.t.wait(v, t_lb)
            v.tensor_copy(d["lgr"][:], ck_ps[b][0][:, 0:128])
            v.tensor_scalar(d["dbuf"][:], d["lgr"][:], lgch[:, 0:1], None,
                            ALU.subtract)
            v.tensor_scalar(d["dl"][:], d["dbuf"][:], -1.0, 0.0, ALU.mult, ALU.min)
            v.tensor_scalar_min(d["du"][:], d["dbuf"][:], 0.0)
        ck_free[b][0] = (ctx.v, ctx.v.tok())
        dd_tok = ctx.v.tok()
        @block.scalar
        def _(s, d=d, dd_tok=dd_tok):
            s = ctx.S(s)
            ctx.v.wait(s, dd_tok)
            s.activation(d["dl"][:], d["dl"][:], AF.Exp)
            s.activation(d["du"][:], d["du"][:], AF.Exp)
        dexp_tok = ctx.s.tok()
        @block.vector
        def _(v, d=d, sl=sl, dexp_tok=dexp_tok):
            v = ctx.V(v)
            ctx.s.wait(v, dexp_tok)
            v.tensor_mul(d["maskD"][sl][:], d["du"][:], cmui[:])
        def f_A(te, ps, d=d, sl=sl, lg_tok=lg_tok):
            ctx.v.wait(te, lg_tok)
            ctx.t.inc(te.matmul(ps[:, 0:128], d["ktT"][sl][:], d["ktT"][sl][:],
                                start=True, stop=True))
        t_A = te_step(b, 1, f_A)
        @block.vector
        def _(v, d=d, b=b, t_A=t_A, dexp_tok=dexp_tok, bech=bech):
            v = ctx.V(v)
            ctx.t.wait(v, t_A)
            ctx.s.wait(v, dexp_tok)
            v.tensor_mul(d["t1"][:], ck_ps[b][1][:, 0:128], d["dl"][:])
            v.tensor_mul(d["t1"][:], d["t1"][:], cmsl[:])
            v.tensor_scalar(d["Nb"][0][:], d["t1"][:], bech, -1.0, ALU.mult, ALU.mult)
        ck_free[b][1] = (ctx.v, ctx.v.tok())
        N_tok = ctx.v.tok()
        def f_NT(te, ps, d=d, N_tok=N_tok):
            ctx.v.wait(te, N_tok)
            ctx.t.inc(te.transpose(ps[:, 0:128], d["Nb"][0][:], identf[:]))
        t_NT = te_step(b, 1, f_NT)
        @block.vector
        def _(v, d=d, b=b, sl=sl, lgch=lgch, bech=bech, v_c=v_c, ktm_c=ktm_c,
              gam_tok=gam_tok, t_NT=t_NT):
            v = ctx.V(v)
            ctx.t.wait(v, t_NT)
            ctx.s.wait(v, gam_tok)
            v.tensor_copy(d["Tb"][0][:], ck_ps[b][1][:, 0:128])
            v.tensor_mul(d["bg"][:, 0:1], bech, lgch[:, 2:3])
            v.tensor_scalar_mul(d["U2"][sl][:, 0:128], v_c, bech)
            v.tensor_scalar_mul(d["U2"][sl][:, 128:256], ktm_c, d["bg"][:, 0:1])
        ck_free[b][1] = (ctx.v, ctx.v.tok())
        U_tok = ctx.v.tok()
        pi = 0
        for it in range(7):
            def f_P(te, ps, d=d, sl=sl, pi=pi, U_tok=U_tok):
                ctx.v.wait(te, U_tok)
                ctx.t.inc(te.matmul(ps[:, 0:256], d["Tb"][pi][:], d["U2"][sl][:],
                                    start=True, stop=True))
            te_step(b, 0, f_P)
            if it < 6:
                def f_NN(te, ps, d=d, pi=pi):
                    te.matmul(ps[:, 0:128], d["Tb"][pi][:], d["Nb"][pi][:],
                              start=True, stop=True)
                    ctx.t.inc(te.matmul(ps[:, 128:256], d["Nb"][pi][:], d["Tb"][pi][:],
                                        start=True, stop=True))
                te_step(b, 1, f_NN)
            tk = ctx.t.tok()
            @block.vector
            def _(v, d=d, b=b, sl=sl, pi=pi, it=it, tk=tk):
                v = ctx.V(v)
                ctx.t.wait(v, tk)
                v.tensor_add(d["U2"][sl][:], d["U2"][sl][:], ck_ps[b][0][:, 0:256])
                if it < 6:
                    v.tensor_copy(d["Nb"][1 - pi][:], ck_ps[b][1][:, 0:128])
                    v.tensor_copy(d["Tb"][1 - pi][:], ck_ps[b][1][:, 128:256])
            U_tok = ctx.v.tok()
            ck_free[b][0] = (ctx.v, U_tok)
            ck_free[b][1] = (ctx.v, U_tok)
            pi = 1 - pi
        def f_WT(te, ps, d=d, sl=sl, U_tok=U_tok):
            ctx.v.wait(te, U_tok)
            te.transpose(ps[:, 0:128], d["U2"][sl][:, 128:256], identf[:])
            ctx.t.inc(te.transpose(ps[:, 128:256], d["U2"][sl][:, 0:128], identf[:]))
        t_WT = te_step(b, 1, f_WT)
        @block.vector
        def _(v, d=d, b=b, sl=sl, t_WT=t_WT):
            v = ctx.V(v)
            ctx.t.wait(v, t_WT)
            v.tensor_copy(d["WrT"][sl][:], ck_ps[b][1][:, 0:128])
            v.tensor_copy(d["t2"][:], ck_ps[b][1][:, 128:256])
        ck_free[b][1] = (ctx.v, ctx.v.tok())
        uvt_tok = ctx.v.tok()
        def f_Gv(te, ps, d=d, qT_c=qT_c, uvt_tok=uvt_tok):
            ctx.v.wait(te, uvt_tok)
            ctx.t.inc(te.matmul(ps[:, 0:128], d["t2"][:], qT_c,
                                start=True, stop=True))
        t_Gv = te_step(b, 0, f_Gv)
        @block.vector
        def _(v, d=d, b=b, sl=sl, t_Gv=t_Gv):
            v = ctx.V(v)
            ctx.t.wait(v, t_Gv)
            v.tensor_copy(d["Gv"][sl][:], ck_ps[b][0][:, 0:128])
        ck_free[b][0] = (ctx.v, ctx.v.tok())
        offready[b][ch] = (ctx.v, ctx.v.tok())

    def emit_rchain(ch, b):
        d = pb[b]
        sl = ch % 4
        tg = b * 16 + ch
        lgch = d["lgch"][sl]
        ktm_c = Ktm[:, tg * 128:(tg + 1) * 128]
        Rc = d["R"][cur[b]]; RTc = d["RT"][cur[b]]
        Rn_ = d["R"][(cur[b] + 1) % NRS]; RTn = d["RT"][(cur[b] + 1) % NRS]
        @block.tensor
        def _(te, d=d, sl=sl, b=b, ch=ch, Rc=Rc):
            gwait(te, offready[b][ch])
            gwait(te, rch_free[b])
            ctx.t.inc(te.matmul(pj_ps[b][:, 0:128], d["WrT"][sl][:], Rc[:],
                                start=True, stop=True))
        wr_tok = ctx.t.tok()
        @block.vector
        def _(v, d=d, sl=sl, b=b, lgch=lgch, wr_tok=wr_tok):
            v = ctx.V(v)
            ctx.t.wait(v, wr_tok)
            v.tensor_sub(d["Ub"][:], d["U2"][sl][:, 0:128], pj_ps[b][:, 0:128])
            v.tensor_scalar_mul(d["Us"][:], d["Ub"][:], lgch[:, 3:4])
        us_tok = ctx.v.tok()
        @block.tensor
        def _(te, d=d, b=b, ktm_c=ktm_c, us_tok=us_tok):
            ctx.v.wait(te, us_tok)
            te.matmul(pj_ps[b][:, 128:256], ktm_c, d["Us"][:],
                      start=True, stop=True)
            ctx.t.inc(te.matmul(pj_ps[b][:, 256:384], d["Us"][:], ktm_c,
                                start=True, stop=True))
        rn_tok = ctx.t.tok()
        @block.vector
        def _(v, d=d, b=b, ch=ch, lgch=lgch, Rc=Rc, RTc=RTc, Rn_=Rn_, RTn=RTn,
              rn_tok=rn_tok):
            v = ctx.V(v)
            ctx.t.wait(v, rn_tok)
            if ch >= 2 and ucons[b][ch - 2] is not None:
                gwait(v, ucons[b][ch - 2])
            v.scalar_tensor_tensor(Rn_[:], Rc[:], lgch[:, 4:5],
                                   pj_ps[b][:, 128:256], ALU.mult, ALU.add)
            v.scalar_tensor_tensor(RTn[:], RTc[:], lgch[:, 4:5],
                                   pj_ps[b][:, 256:384], ALU.mult, ALU.add)
        rch_free[b] = (ctx.v, ctx.v.tok())
        onready[b][ch] = (ctx.v, ctx.v.tok())
        cur[b] = (cur[b] + 1) % NRS

    def emit_outpath(ch, b):
        d = pb[b]
        sl = ch % 4
        tg = b * 16 + ch
        lgch = d["lgch"][sl]
        rslot = rsnap[b][ch]
        ktm_c = Ktm[:, tg * 128:(tg + 1) * 128]
        g_c = Gt[:, tg * 128:(tg + 1) * 128]
        qT_c = QT[:, tg * 128:(tg + 1) * 128]
        Rc = d["R"][rslot]; RTc = d["RT"][rslot]
        @block.tensor
        def _(te, d=d, b=b, ch=ch, qT_c=qT_c, Rc=Rc, RTc=RTc):
            gwait(te, out_free[b])
            gwait(te, onready[b][ch])
            te.matmul(out_ps[b][:, 0:128], RTc[:], qT_c, start=True, stop=True)
            ctx.t.inc(te.matmul(out_ps[b][:, 128:256], qT_c, RTc[:],
                                start=True, stop=True))
        ucons[b][ch] = (ctx.t, ctx.t.tok())
        rq_tok = ctx.t.tok()
        @block.vector
        def _(v, d=d, b=b, rq_tok=rq_tok):
            v = ctx.V(v)
            ctx.t.wait(v, rq_tok)
            v.tensor_copy(d["rq"][:], out_ps[b][:, 0:128])
        rq2_tok = ctx.v.tok()
        @block.tensor
        def _(te, d=d, sl=sl, b=b, rq2_tok=rq2_tok):
            ctx.v.wait(te, rq2_tok)
            ctx.t.inc(te.matmul(out_ps[b][:, 256:384], d["WrT"][sl][:], d["rq"][:],
                                start=True, stop=True))
        gr_tok = ctx.t.tok()
        @block.vector
        def _(v, d=d, sl=sl, b=b, gr_tok=gr_tok):
            v = ctx.V(v)
            ctx.t.wait(v, gr_tok)
            v.tensor_sub(d["t1"][:], d["Gv"][sl][:], out_ps[b][:, 256:384])
            v.tensor_mul(d["WtT"][:], d["t1"][:], d["maskD"][sl][:])
        wt_tok = ctx.v.tok()
        @block.tensor
        def _(te, d=d, b=b, ktm_c=ktm_c, wt_tok=wt_tok):
            ctx.v.wait(te, wt_tok)
            ctx.t.inc(te.matmul(out_ps[b][:, 256:384], d["WtT"][:], ktm_c,
                                start=True, stop=True))
        oi_tok = ctx.t.tok()
        @block.vector
        def _(v, d=d, b=b, sl=sl, lgch=lgch, g_c=g_c, oi_tok=oi_tok):
            v = ctx.V(v)
            ctx.t.wait(v, oi_tok)
            v.tensor_scalar_mul(d["t1"][:], out_ps[b][:, 128:256], lgch[:, 2:3])
            v.tensor_add(d["t1"][:], d["t1"][:], out_ps[b][:, 256:384])
            v.tensor_mul(d["OC"][:], d["t1"][:], g_c)
        oc_tok = ctx.v.tok()
        @block.tensor
        def _(te, d=d, b=b, oc_tok=oc_tok):
            ctx.v.wait(te, oc_tok)
            ctx.t.inc(te.transpose(out_ps[b][:, 0:128], d["OC"][:], identf[:]))
        ot_tok = ctx.t.tok()
        @block.vector
        def _(v, d=d, b=b, tg=tg, ot_tok=ot_tok):
            v = ctx.V(v)
            ctx.t.wait(v, ot_tok)
            v.tensor_copy(oallTC[:, tg * 128:(tg + 1) * 128], out_ps[b][:, 0:128])
        out_free[b] = (ctx.v, ctx.v.tok())

    for ch in range(16):
        if ch == 0:
            for b in range(2):
                emit_off(0, b)
            for b in range(2):
                emit_off(1, b)
        if ch + 2 < 16:
            for b in range(2):
                emit_off(ch + 2, b)
        for b in range(2):
            rsnap[b][ch] = cur[b]
            emit_rchain(ch, b)
        if ch > 0:
            for b in range(2):
                emit_outpath(ch - 1, b)
    for b in range(2):
        emit_outpath(15, b)

    yst = [ctx.gst[2], ctx.gst[3]]
    scan_done = ctx.v.tok()
    ev_toks = []
    obank = [[out_ps[0], out_ps[1]], [pj_ps[0], pj_ps[1]]]
    for tt in range(32):
        pp = tt % 2
        @block.tensor
        def _(te, tt=tt, pp=pp):
            if tt == 0:
                ctx.v.wait(te, scan_done)
            if tt >= 2:
                ctx.v.wait(te, ev_toks[tt - 2])
            for chh in range(2):
                ctx.t.inc(te.matmul(obank[pp][chh][:, 0:512],
                                    oallTC[:, tt * 128:(tt + 1) * 128],
                                    wob[:, chh * 512:(chh + 1) * 512],
                                    start=True, stop=True))
        mm_tok = ctx.t.tok()
        @block.vector
        def _(v, tt=tt, pp=pp, mm_tok=mm_tok):
            v = ctx.V(v)
            ctx.t.wait(v, mm_tok)
            yst[tt % 2].wait(v)
            for chh in range(2):
                v.tensor_add(ybuf[tt % 2][:, chh * 512:(chh + 1) * 512],
                             obank[pp][chh][:, 0:512],
                             bo_bc[:, chh * 512:(chh + 1) * 512])
            v.bn_stats(stats[:, tt * 12:tt * 12 + 6], ybuf[tt % 2][:, 0:512])
            v.bn_stats(stats[:, tt * 12 + 6:tt * 12 + 12], ybuf[tt % 2][:, 512:1024])
            ev_toks.append(ctx.v.tok())
        @block.gpsimd
        def _(g, tt=tt):
            ctx.v.wait(g, ev_toks[-1])
            yst[tt % 2].inc(g.dma_start(out=scr_y[tt * 128:(tt + 1) * 128, :],
                                        in_=ybuf[tt % 2][:]), 16)

    if do_epi:
        @block.sync
        def _(q):
            for p in range(2):
                yst[p].wait(q)
        return emit_ln_residual_epilogue(ctx, scr_y, x_hbm, out_hbm, mean, var,
                                         rstd, xbuf, ybuf, obuf, stats,
                                         stats_ready=True, out_stats=out_stats)
    return None


def build_phaseC_program(**kw):
    nc = bass.Bass("TRN2", target_bir_lowering=False, debug=False)
    x = nc.dram_tensor("x", [TOK, D], F32, kind="ExternalInput").ap()
    names = [("Wq", [D, 128]), ("bq", [128]), ("Wk", [D, 128]), ("bk", [128]),
             ("Wv", [D, 128]), ("bv", [128]), ("Wab", [D, 2]), ("bab", [2]),
             ("Wg", [D, 128]), ("bg", [128]), ("Wo", [128, D]), ("bo", [D])]
    W = {nm: nc.dram_tensor(nm, sh, F32, kind="ExternalInput").ap() for nm, sh in names}
    ident = nc.dram_tensor("ident", [128, 128], F32, kind="ExternalInput").ap()
    onesf = nc.dram_tensor("onesf", [1, 128], F32, kind="ExternalInput").ap()
    ccum = nc.dram_tensor("ccum", [128, 128], F32, kind="ExternalInput").ap()
    csuf = nc.dram_tensor("csuf", [128, 128], F32, kind="ExternalInput").ap()
    cmsl = nc.dram_tensor("cmsl", [128, 128], F32, kind="ExternalInput").ap()
    cmui = nc.dram_tensor("cmui", [128, 128], F32, kind="ExternalInput").ap()
    out = nc.dram_tensor("out", [TOK, D], F32, kind="ExternalOutput").ap()
    scr_n = nc.dram_tensor("scr_n", [TOK, D], BF16, kind="Internal").ap()
    scr_y = nc.dram_tensor("scr_y", [TOK, D], F32, kind="Internal").ap()
    with nc.Block() as block:
        ctx = Ctx(nc, block)
        ctx.ident_hbm = ident
        ctx.onesf_hbm = onesf
        ctx.ccum_hbm = ccum
        ctx.csuf_hbm = csuf
        ctx.cmsl_hbm = cmsl
        ctx.cmui_hbm = cmui
        st = emit_phaseC(ctx, x, W, out, scr_n, scr_y, **kw)
        @block.sync
        def _(q):
            if st is not None:
                for p in range(2):
                    st[p].wait(q)
            ctx.v.wait(q)
            ctx.s.wait(q)
            ctx.t.wait(q)
    return nc


def build_cmats():
    r = np.arange(128)
    ccum = (r[:, None] <= r[None, :]).astype(np.float32)
    csuf = (r[:, None] > r[None, :]).astype(np.float32)
    cmsl = (r[:, None] > r[None, :]).astype(np.float32)
    cmui = (r[None, :] >= r[:, None]).astype(np.float32)
    return ccum, csuf, cmsl, cmui




S128 = 1.0 / np.sqrt(128.0)


def emit_phaseD(ctx, x_hbm, W, out_hbm, scr_n, scr_y, scr_xf, scr_hs, scr_int,
                do_epi=True, in_stats=None, out_stats=None):
    nc, block = ctx.nc, ctx.block
    xbuf = [ctx.sbuf("D_x0", [128, 1024], F32), ctx.sbuf("D_x1", [128, 1024], F32)]
    nbuf = [ctx.sbuf("D_n0", [128, 1024], BF16), ctx.sbuf("D_n1", [128, 1024], BF16)]
    ybuf = [ctx.sbuf("D_y0", [128, 1024], F32), ctx.sbuf("D_y1", [128, 1024], F32)]
    obuf = [ctx.sbuf("D_o0", [128, 1024], F32), ctx.sbuf("D_o1", [128, 1024], F32)]
    fbuf = [ctx.sbuf("D_f0", [128, 1024], F32), ctx.sbuf("D_f1", [128, 1024], F32)]
    stats = ctx.sbuf("D_stats", [128, NTILE * 12 + 2 * NTILE], F32)
    mean = ctx.sbuf("D_mean", [128, NTILE], F32)
    var = ctx.sbuf("D_var", [128, NTILE], F32)
    rstd = ctx.sbuf("D_rstd", [128, NTILE], F32)
    identf = ctx.sbuf("D_idf", [128, 128], F32)
    identb = ctx.sbuf("D_idb", [128, 128], BF16)
    onesb = ctx.sbuf("D_ones", [128, 128], BF16)
    onesf1 = ctx.sbuf("D_onesf1", [1, 128], F32)
    lnw1 = ctx.sbuf("D_lnw1", [128, 1024], F32)
    lnb1 = ctx.sbuf("D_lnb1", [128, 1024], F32)
    lnw2 = ctx.sbuf("D_lnw2", [128, 1024], F32)
    lnb2 = ctx.sbuf("D_lnb2", [128, 1024], F32)
    rn_sb = ctx.sbuf("D_rn", [128, 32], F32)
    rn2_sb = ctx.sbuf("D_rn2", [128, 32], F32)
    simb = ctx.sbuf("D_simb", [128, 64], BF16)
    simb2 = ctx.sbuf("D_simb2", [128, 64], BF16)
    rsimb = ctx.sbuf("D_rsimb", [128, 8], F32)
    sgb = ctx.sbuf("D_sgb", [128, 8], F32)
    rsimb2 = ctx.sbuf("D_rsimb2", [128, 8], F32)
    sgb2 = ctx.sbuf("D_sgb2", [128, 8], F32)
    cselb = ctx.sbuf("D_csel", [8, 1024], BF16)
    gk = ["lbuf", "gbuf", "mbuf", "obuf8", "m2buf", "oh2", "gm", "eb", "cols", "rw"]
    gwork = {k: ctx.sbuf("D_g_" + k, [128, 8], F32) for k in gk}
    gwork["ss8"] = ctx.sbuf("D_gss8", [8, 8], F32)
    gwork["sse"] = ctx.sbuf("D_gsse", [8, 2], F32)
    gwork["grow"] = ctx.sbuf("D_grow", [1, 24], F32)
    nT = [ctx.sbuf("D_nT0", [128, 8 * 512], BF16), ctx.sbuf("D_nT1", [128, 8 * 512], BF16)]
    rw2T = ctx.sbuf("D_rw2T", [8, 4096], BF16)
    bcb = [ctx.sbuf("D_bcb0", [128, 512], BF16), ctx.sbuf("D_bcb1", [128, 512], BF16)]
    ps = [ctx.psum("D_ps%d" % k) for k in range(8)]
    m1 = ctx.mark()
    cmkb = ctx.sbuf("D_cmk", [128, 2048], BF16)
    rwT = ctx.sbuf("D_rwT", [8, 4096], BF16)
    QTm = ctx.sbuf("D_QTm", [128, 4096], BF16)
    KTm = ctx.sbuf("D_KTm", [128, 4096], BF16)
    VTm = ctx.sbuf("D_VTm", [128, 4096], BF16)
    Vtm = ctx.sbuf("D_Vtm", [128, 4096], BF16)
    oTd = ctx.sbuf("D_oTd", [128, 4096], BF16)
    qkvw = [ctx.sbuf("D_qkvw%d" % i, [128, 3 * 1024], BF16) for i in range(2)]
    opb = ctx.sbuf("D_opb", [128, 8192], BF16)
    neb = [ctx.sbuf("D_neb0", [128, 8 * 512], BF16), ctx.sbuf("D_neb1", [128, 8 * 512], BF16)]
    q2d = ctx.sbuf("D_q2", [128, 2048], BF16)
    nrmD = ctx.sbuf("D_nrm", [1, 520], F32)
    ebiasD = ctx.sbuf("D_ebias", [128, 1], F32)
    expbD = [ctx.sbuf("D_e%d" % k, [128, 512], BF16) for k in range(4)]
    drecD = ctx.sbuf("D_dr", [128, 512], F32)
    opmix = [ctx.sbuf("D_opmix%d" % k, [128, 512], BF16) for k in range(2)]
    bcall = ctx.sbuf("D_bcall", [128, 8 * 512], BF16)

    wsem = ctx.dsem("Dw", swdge=True)
    csem = ctx.dsem("Dc")
    @block.gpsimd
    def _(g):
        wsem.inc(g.dma_start(out=opb[:].rearrange("p (e m) -> p e m", e=8),
                             in_=W["o_proj"].rearrange("e p m -> p e m")), 16)
        wsem.inc(g.dma_start(out=simb[:].rearrange("p (dc m) -> p dc m", dc=8),
                             in_=W["smha_sim"].rearrange("(dc p) m -> p dc m", p=128)), 16)
        wsem.inc(g.dma_start(out=simb2[:].rearrange("p (dc m) -> p dc m", dc=8),
                             in_=W["moe_sim"].rearrange("(dc p) m -> p dc m", p=128)), 16)
    @block.sync
    def _(q):
        csem.inc(q.dma_start(out=identf[:], in_=ctx.ident_hbm[:, :]), 16)
        csem.inc(q.dma_start(out=identb[:], in_=ctx.identb_hbm[:, :]), 16)
        csem.inc(q.dma_start(out=onesb[:], in_=ctx.ones_hbm[:, :]), 16)
        csem.inc(q.dma_start(out=onesf1[:], in_=ctx.onesf_hbm[:, :]), 16)
        csem.inc(q.dma_start(out=cselb[:], in_=ctx.csel_hbm[:, :]), 16)
        csem.inc(q.dma_start(out=cmkb[:], in_=ctx.cmk_hbm[:, :]), 16)
        for nm, buf in [("ln1_w", lnw1), ("ln1_b", lnb1), ("ln2_w", lnw2), ("ln2_b", lnb2)]:
            csem.inc(q.dma_start(out=buf[:], in_=W[nm][None, :].partition_broadcast(128)), 16)

    if in_stats is not None:
        s_tok = ln_stats_batch_finish(ctx, in_stats, mean, var, rstd,
                                      in_stats[0:128, NTILE * 12:NTILE * 14])
    else:
        s_tok = emit_stats_pass(ctx, x_hbm, mean, var, rstd, xbuf, stats)
    ld1 = [ctx.dsem("ld"), ctx.dsem("ld")]
    st1 = [ctx.gst[2], ctx.gst[3]]
    v1 = []
    for i in range(NTILE):
        @block.sync
        def _(q, i=i):
            if i < 2:
                ctx.v.wait(q)
            else:
                ctx.v.wait(q, v1[i - 2])
            ld1[i % 2].inc(q.dma_start(out=xbuf[i % 2][:, 0:512],
                                       in_=x_hbm[i * 128:(i + 1) * 128, 0:512]), 16)
        @block.scalar
        def _(q, i=i):
            if i < 2:
                ctx.v.wait(q)
            else:
                ctx.v.wait(q, v1[i - 2])
            ld1[i % 2].inc(q.dma_start(out=xbuf[i % 2][:, 512:1024],
                                       in_=x_hbm[i * 128:(i + 1) * 128, 512:1024]), 16)
        @block.vector
        def _(v, i=i):
            v = ctx.V(v)
            ld1[i % 2].wait(v)
            ctx.s.wait(v, s_tok)
            if i >= 2:
                st1[i % 2].wait(v)
            v.tensor_scalar(fbuf[i % 2][:], xbuf[i % 2][:],
                            mean[:, i:i + 1], rstd[:, i:i + 1],
                            ALU.subtract, ALU.mult)
            v.bn_stats(stats[:, i * 12:i * 12 + 6], fbuf[i % 2][:, 0:512])
            v.bn_stats(stats[:, i * 12 + 6:i * 12 + 12], fbuf[i % 2][:, 512:1024])
            v1.append(ctx.v.tok())
        @block.gpsimd
        def _(g, i=i):
            ctx.v.wait(g, v1[i])
            st1[i % 2].inc(g.dma_start(out=scr_xf[i * 128:(i + 1) * 128, :],
                                       in_=fbuf[i % 2][:]), 16)
    emit_phase_barrier(ctx, extra_dsems=st1)

    def weighted_ln(src, lnw, lnb, rn_out, chunk_cb=None, stats_ready=False):
        if stats_ready:
            s_tok = ln_stats_batch_finish(ctx, stats, mean, var, rstd,
                                          stats[0:128, NTILE * 12:NTILE * 14])
        else:
            s_tok = emit_stats_pass(ctx, src, mean, var, rstd, xbuf, stats)
        ld = [ctx.dsem("ld"), ctx.dsem("ld")]
        st = [ctx.dsem("wst", swdge=True), ctx.dsem("wst", swdge=True)]
        vtk = []
        g_nb = [None, None]
        for i in range(NTILE):
            @block.sync
            def _(q, i=i):
                if i < 2:
                    ctx.v.wait(q)
                else:
                    ctx.v.wait(q, vtk[i - 2])
                ld[i % 2].inc(q.dma_start(out=xbuf[i % 2][:, 0:512],
                                          in_=src[i * 128:(i + 1) * 128, 0:512]), 16)
            @block.scalar
            def _(q, i=i):
                if i < 2:
                    ctx.v.wait(q)
                else:
                    ctx.v.wait(q, vtk[i - 2])
                ld[i % 2].inc(q.dma_start(out=xbuf[i % 2][:, 512:1024],
                                          in_=src[i * 128:(i + 1) * 128, 512:1024]), 16)
            @block.vector
            def _(v, i=i):
                v = ctx.V(v)
                if i == 0:
                    csem.wait(v)
                ld[i % 2].wait(v)
                ctx.s.wait(v, s_tok)
                if i >= 2:
                    st[i % 2].wait(v)
                v.tensor_scalar(fbuf[i % 2][:], xbuf[i % 2][:],
                                mean[:, i:i + 1], rstd[:, i:i + 1],
                                ALU.subtract, ALU.mult)
                v.tensor_mul(fbuf[i % 2][:], fbuf[i % 2][:], lnw[:])
                v.tensor_add(nbuf[i % 2][:], fbuf[i % 2][:], lnb[:])
                g_nb[i % 2] = (ctx.v, ctx.v.tok())
                v.tensor_mul(obuf[i % 2][:], nbuf[i % 2][:], nbuf[i % 2][:])
                v.reduce_sum(rn_out[:, i:i + 1], obuf[i % 2][:], axis=AX.X)
                vtk.append(ctx.v.tok())
            @block.gpsimd
            def _(g, i=i):
                gwait(g, g_nb[i % 2])
                st[i % 2].inc(g.dma_start(out=scr_n[i * 128:(i + 1) * 128, :],
                                          in_=nbuf[i % 2][:]), 16)
            if chunk_cb is not None and i % 4 == 3 and i // 4 < 2:
                chunk_cb(i // 4, st)
        @block.scalar
        def _(s):
            s = ctx.S(s)
            ctx.v.wait(s)
            s.activation(rn_out[:, 0:NTILE], rn_out[:, 0:NTILE], AF.Ln)
            s.activation(rn_out[:, 0:NTILE], rn_out[:, 0:NTILE], AF.Exp, scale=-0.5)
        rr_tok = ctx.s.tok()
        @block.vector
        def _(v, rr_tok=rr_tok):
            v = ctx.V(v)
            ctx.s.wait(v, rr_tok)
            v.tensor_scalar_min(rn_out[:, 0:NTILE], rn_out[:, 0:NTILE], 1e12)
        return st

    nTs = [ctx.dsem("DnT"), ctx.dsem("DnT")]
    nT_done = [None, None]

    def _D4_load_chunk(tc, stsems):
        pn = tc % 2
        @block.sync
        def _(q, tc=tc, pn=pn, stsems=stsems):
            for p2 in range(2):
                stsems[p2].wait(q)
            if tc >= 2:
                ctx.t.wait(q, nT_done[tc % 2])
            for dc in range(8):
                nTs[pn].inc(q.dma_start(out=nT[pn][:, dc * 512:(dc + 1) * 512],
                                        in_=scr_n[tc * 512:(tc + 1) * 512,
                                                  dc * 128:(dc + 1) * 128],
                                        transpose=True), 16)

    st_n1 = weighted_ln(scr_xf, lnw1, lnb1, rn_sb, chunk_cb=_D4_load_chunk,
                        stats_ready=True)

    def sim_prep(simx, gates_hbm, rsimx, sgx):
        gsem = ctx.dsem("Dg")
        grow = gwork["grow"]
        @block.sync
        def _(q):
            ctx.v.wait(q)
            gsem.inc(q.dma_start(out=grow[0:1, 0:8], in_=gates_hbm[None, :]), 16)
        @block.tensor
        def _(te):
            wsem.wait(te)
            csem.wait(te)
            ctx.v.wait(te)
            for dc in range(8):
                ins = te.matmul(ps[6][0:8, 0:8], simx[:, dc * 8:(dc + 1) * 8],
                                simx[:, dc * 8:(dc + 1) * 8], start=(dc == 0),
                                stop=(dc == 7))
            ctx.t.inc(ins)
        mm_tok = ctx.t.tok()
        @block.vector
        def _(v, mm_tok=mm_tok):
            v = ctx.V(v)
            ctx.t.wait(v, mm_tok)
            gsem.wait(v)
            v.tensor_mul(gwork["ss8"][0:8, 0:8], ps[6][0:8, 0:8], identf[0:8, 0:8])
            v.reduce_sum(gwork["sse"][0:8, 0:1], gwork["ss8"][0:8, 0:8], axis=AX.X)
            v.tensor_scalar_mul(grow[0:1, 8:16], grow[0:1, 0:8], -1.0)
        ss_tok = ctx.v.tok()
        @block.scalar
        def _(s, ss_tok=ss_tok):
            s = ctx.S(s)
            ctx.v.wait(s, ss_tok)
            s.activation(gwork["sse"][0:8, 1:2], gwork["sse"][0:8, 0:1], AF.Ln)
            s.activation(gwork["sse"][0:8, 1:2], gwork["sse"][0:8, 1:2], AF.Exp,
                         scale=-0.5)
            s.activation(grow[0:1, 8:16], grow[0:1, 8:16], AF.Exp)
        rs_tok = ctx.s.tok()
        @block.vector
        def _(v, rs_tok=rs_tok):
            v = ctx.V(v)
            ctx.s.wait(v, rs_tok)
            v.tensor_scalar_min(gwork["sse"][0:8, 1:2], gwork["sse"][0:8, 1:2], 1e12)
            v.tensor_scalar_add(grow[0:1, 8:16], grow[0:1, 8:16], 1.0)
            v.reciprocal(grow[0:1, 8:16], grow[0:1, 8:16])
        sg_tok = ctx.v.tok()
        @block.tensor
        def _(te, sg_tok=sg_tok):
            ctx.v.wait(te, sg_tok)
            ctx.t.inc(te.transpose(ps[6][0:1, 0:8], gwork["sse"][0:8, 1:2],
                                   identf[0:8, 0:8]))
        tr_tok = ctx.t.tok()
        @block.vector
        def _(v, tr_tok=tr_tok):
            v = ctx.V(v)
            ctx.t.wait(v, tr_tok)
            v.tensor_copy(grow[0:1, 16:24], ps[6][0:1, 0:8])
        rr_tok = ctx.v.tok()
        @block.tensor
        def _(te, rr_tok=rr_tok):
            ctx.v.wait(te, rr_tok)
            te.matmul(ps[6][:, 0:8], onesf1[0:1, :], grow[0:1, 16:24],
                      start=True, stop=True)
            ctx.t.inc(te.matmul(ps[6][:, 8:16], onesf1[0:1, :], grow[0:1, 8:16],
                                start=True, stop=True))
        bc_tok = ctx.t.tok()
        @block.vector
        def _(v, bc_tok=bc_tok):
            v = ctx.V(v)
            ctx.t.wait(v, bc_tok)
            v.tensor_copy(rsimx[:], ps[6][:, 0:8])
            v.tensor_copy(sgx[:], ps[6][:, 8:16])

    sim_prep(simb, W["smha_gates"], rsimb, sgb)
    sim_prep(simb2, W["moe_gates"], rsimb2, sgb2)

    def gating_tile(raw_ps_ap, rn_col, rsimx, sgx, dst_rwT, tg, guard_list):
        @block.vector
        def _(v):
            v = ctx.V(v)
            ctx.t.wait(v)
            v.tensor_scalar_mul(gwork["lbuf"][:], raw_ps_ap, rn_col)
            v.tensor_mul(gwork["lbuf"][:], gwork["lbuf"][:], rsimx[:])
            v.tensor_sub(gwork["lbuf"][:], gwork["lbuf"][:], sgx[:])
            v.tensor_scalar_max(gwork["gbuf"][:], gwork["lbuf"][:], 0.0)
            v.tensor_single_scalar(gwork["mbuf"][:], gwork["lbuf"][:], 0.0, ALU.is_gt)
            v.reduce_sum(gwork["cols"][:, 0:1], gwork["mbuf"][:], axis=AX.X)
            v.tensor_single_scalar(gwork["cols"][:, 0:1], gwork["cols"][:, 0:1],
                                   0.0, ALU.is_equal)
            v.reduce_max(gwork["cols"][:, 1:2], gwork["lbuf"][:], axis=AX.X)
            v.tensor_scalar(gwork["obuf8"][:], gwork["lbuf"][:],
                            gwork["cols"][:, 1:2], None, ALU.is_ge)
            v.scalar_tensor_tensor(gwork["m2buf"][:], gwork["obuf8"][:], -1e30,
                                   gwork["lbuf"][:], ALU.mult, ALU.add)
            v.reduce_max(gwork["cols"][:, 2:3], gwork["m2buf"][:], axis=AX.X)
            v.tensor_scalar(gwork["oh2"][:], gwork["m2buf"][:],
                            gwork["cols"][:, 2:3], None, ALU.is_ge)
            v.tensor_add(gwork["obuf8"][:], gwork["obuf8"][:], gwork["oh2"][:])
            v.tensor_scalar_mul(gwork["obuf8"][:], gwork["obuf8"][:],
                                gwork["cols"][:, 0:1])
            v.tensor_add(gwork["mbuf"][:], gwork["mbuf"][:], gwork["obuf8"][:])
            v.tensor_mul(gwork["gm"][:], gwork["gbuf"][:], gwork["mbuf"][:])
            v.reduce_max(gwork["cols"][:, 3:4], gwork["gm"][:], axis=AX.X)
            v.tensor_scalar_mul(gwork["cols"][:, 3:4], gwork["cols"][:, 3:4], -1.0)
        guard_list[0] = (ctx.v, ctx.v.tok())
        mx_tok = ctx.v.tok()
        @block.scalar
        def _(s, mx_tok=mx_tok):
            s = ctx.S(s)
            ctx.v.wait(s, mx_tok)
            s.activation(gwork["eb"][:], gwork["gbuf"][:], AF.Exp,
                         bias=gwork["cols"][:, 3:4])
        e_tok = ctx.s.tok()
        @block.vector
        def _(v, e_tok=e_tok):
            v = ctx.V(v)
            ctx.s.wait(v, e_tok)
            v.tensor_mul(gwork["eb"][:], gwork["eb"][:], gwork["mbuf"][:])
            v.reduce_sum(gwork["cols"][:, 4:5], gwork["eb"][:], axis=AX.X)
            v.reciprocal(gwork["cols"][:, 5:6], gwork["cols"][:, 4:5])
            v.tensor_scalar_mul(gwork["rw"][:], gwork["eb"][:], gwork["cols"][:, 5:6])
        rw_tok = ctx.v.tok()
        @block.tensor
        def _(te, rw_tok=rw_tok):
            ctx.v.wait(te, rw_tok)
            gwait(te, guard_list[1])
            ctx.s.wait(te)
            ctx.t.inc(te.transpose(ps[4][0:8, 0:128], gwork["rw"][:], identf[:]))
        tr_tok = ctx.t.tok()
        @block.vector
        def _(v, tg=tg, tr_tok=tr_tok, dst_rwT=dst_rwT):
            v = ctx.V(v)
            ctx.t.wait(v, tr_tok)
            v.tensor_copy(dst_rwT[0:8, tg * 128:(tg + 1) * 128], ps[4][0:8, 0:128])
        guard_list[1] = (ctx.v, ctx.v.tok())

    qkvsem = [ctx.dsem("Dqw", swdge=True), ctx.dsem("Dqw", swdge=True)]
    gguard = [None, None]
    qkv_ev = None
    bc_g = [None, None]
    bcb_g = [[None, None], [None, None]]
    neb_g = [None, None]
    qkvw_g = [None, None]
    for tc in range(8):
        pn = tc % 2
        if tc >= 2:
            _D4_load_chunk(tc, st_n1)
        for i in range(4):
            tg = tc * 4 + i
            @block.tensor
            def _(te, tc=tc, pn=pn, i=i):
                nTs[pn].wait(te)
                gwait(te, gguard[0])
                for dc in range(8):
                    ins = te.matmul(ps[5][:, 0:8],
                                    nT[pn][:, dc * 512 + i * 128: dc * 512 + (i + 1) * 128],
                                    simb[:, dc * 8:(dc + 1) * 8],
                                    start=(dc == 0), stop=(dc == 7))
                ctx.t.inc(ins)
            gating_tile(ps[5][:, 0:8], rn_sb[:, tg:tg + 1], rsimb, sgb, rwT, tg, gguard)
        rwT_tok = ctx.v.tok()
        for e in range(8):
            pe = e % 2
            bk6 = ps[6] if pe == 0 else ps[4]
            @block.gpsimd
            def _(g, tc=tc, e=e, pe=pe):
                gwait(g, qkvw_g[pe])
                for ipp, nm in enumerate(["q_proj", "k_proj", "v_proj"]):
                    qkvsem[pe].inc(g.dma_start(
                        out=qkvw[pe][:, ipp * 1024:(ipp + 1) * 1024]
                            .rearrange("p (dc m) -> p dc m", dc=8),
                        in_=W[nm][e].rearrange("(dc p) m -> p dc m", p=128)), 16)
            @block.tensor
            def _(te, tc=tc, e=e, pe=pe, bk6=bk6, rwT_tok=rwT_tok):
                ctx.v.wait(te, rwT_tok)
                gwait(te, bc_g[pe])
                ctx.t.inc(te.matmul(bk6[:], cselb[:, e * 128:(e + 1) * 128],
                                    rwT[0:8, tc * 512:(tc + 1) * 512],
                                    start=True, stop=True))
            bct_tok = ctx.t.tok()
            @block.vector
            def _(v, pe=pe, bk6=bk6, bct_tok=bct_tok):
                v = ctx.V(v)
                ctx.t.wait(v, bct_tok)
                gwait(v, bcb_g[pe][1])
                v.tensor_copy(bcb[pe][:], bk6[:])
            bc_g[pe] = (ctx.v, ctx.v.tok())
            bcv_tok = ctx.v.tok()
            @block.vector
            def _(v, pn=pn, pe=pe, bcv_tok=bcv_tok):
                v = ctx.V(v)
                gwait(v, neb_g[pe])
                for dc in range(4):
                    v.tensor_mul(neb[pe][:, dc * 512:(dc + 1) * 512],
                                 nT[pn][:, dc * 512:(dc + 1) * 512], bcb[pe][:])
            nev_tok = ctx.v.tok()
            bcb_g[pe][0] = (ctx.v, nev_tok)
            @block.gpsimd
            def _(g, pn=pn, pe=pe, bcv_tok=bcv_tok):
                g = ctx.G(g)
                ctx.v.wait(g, bcv_tok)
                gwait(g, neb_g[pe])
                for dc in range(4, 8):
                    g.tensor_mul(neb[pe][:, dc * 512:(dc + 1) * 512],
                                 nT[pn][:, dc * 512:(dc + 1) * 512], bcb[pe][:])
            neg_tok = ctx.g.tok()
            bcb_g[pe][1] = (ctx.g, neg_tok)
            @block.tensor
            def _(te, tc=tc, e=e, pe=pe, nev_tok=nev_tok, neg_tok=neg_tok):
                ctx.v.wait(te, nev_tok)
                ctx.g.wait(te, neg_tok)
                qkvsem[pe].wait(te)
                if e == 0:
                    gwait(te, qkv_ev)
                for ipp in range(3):
                    for dc in range(8):
                        ins = te.matmul(ps[ipp][:],
                                        qkvw[pe][:, ipp * 1024 + dc * 128:
                                                 ipp * 1024 + (dc + 1) * 128],
                                        neb[pe][:, dc * 512:(dc + 1) * 512],
                                        start=(e == 0 and dc == 0),
                                        stop=(e == 7 and dc == 7))
                ctx.t.inc(ins)
            neb_g[pe] = (ctx.t, ctx.t.tok())
            qkvw_g[pe] = (ctx.t, ctx.t.tok())
        qkv_tok = ctx.t.tok()
        @block.vector
        def _(v, tc=tc, qkv_tok=qkv_tok):
            v = ctx.V(v)
            ctx.t.wait(v, qkv_tok)
            v.tensor_scalar(QTm[:, tc * 512:(tc + 1) * 512], ps[0][:],
                            S128, None, ALU.mult)
            v.tensor_copy(KTm[:, tc * 512:(tc + 1) * 512], ps[1][:])
            v.tensor_copy(VTm[:, tc * 512:(tc + 1) * 512], ps[2][:])
        qkv_ev = (ctx.v, ctx.v.tok())
        nT_done[pn] = ctx.t.tok()
    emit_phase_barrier(ctx)

    vt_g = None
    for tt in range(32):
        @block.tensor
        def _(te, tt=tt):
            gwait(te, vt_g)
            ctx.t.inc(te.transpose(ps[3][:].bitcast(BF16)[:, 0:128],
                                   VTm[:, tt * 128:(tt + 1) * 128], identb[:]))
        tr_tok = ctx.t.tok()
        @block.vector
        def _(v, tt=tt, tr_tok=tr_tok):
            v = ctx.V(v)
            ctx.t.wait(v, tr_tok)
            v.tensor_copy(Vtm[:, tt * 128:(tt + 1) * 128],
                          ps[3][:].bitcast(BF16)[:, 0:128])
        vt_g = (ctx.v, ctx.v.tok())
    emit_phase_barrier(ctx)

    scbD = [ps[0], ps[1], ps[6], ps[7]]
    sc_free = [None, None, None, None]
    expb_free = [None, None, None, None]
    oT_free = [None, None]
    den_free = [None, None]
    for b in range(2):
        t0 = b * 2048
        @block.vector
        def _(v, t0=t0):
            v = ctx.V(v)
            v.tensor_mul(q2d[:], QTm[:, t0:t0 + 2048], QTm[:, t0:t0 + 2048])
        q2_tok = ctx.v.tok()
        @block.tensor
        def _(te, q2_tok=q2_tok):
            ctx.v.wait(te, q2_tok)
            ctx.s.wait(te)
            gwait(te, den_free[0])
            for c in range(4):
                ins = te.matmul(ps[6][0:1, :], onesb[:, 0:1],
                                q2d[:, c * 512:(c + 1) * 512], start=(c == 0),
                                stop=(c == 3))
            ctx.t.inc(ins)
        nq_tok = ctx.t.tok()
        @block.vector
        def _(v, nq_tok=nq_tok):
            v = ctx.V(v)
            ctx.t.wait(v, nq_tok)
            v.reduce_max(nrmD[0:1, 512:513], ps[6][0:1, :], axis=AX.X)
        den_free[0] = (ctx.v, ctx.v.tok())
        @block.vector
        def _(v, t0=t0):
            v = ctx.V(v)
            v.tensor_mul(q2d[:], KTm[:, t0:t0 + 2048], KTm[:, t0:t0 + 2048])
        k2_tok = ctx.v.tok()
        @block.tensor
        def _(te, k2_tok=k2_tok):
            ctx.v.wait(te, k2_tok)
            gwait(te, den_free[0])
            for c in range(4):
                ins = te.matmul(ps[6][0:1, :], onesb[:, 0:1],
                                q2d[:, c * 512:(c + 1) * 512], start=(c == 0),
                                stop=(c == 3))
            ctx.t.inc(ins)
        nk_tok = ctx.t.tok()
        @block.vector
        def _(v, nk_tok=nk_tok):
            v = ctx.V(v)
            ctx.t.wait(v, nk_tok)
            v.reduce_max(nrmD[0:1, 513:514], ps[6][0:1, :], axis=AX.X)
            v.tensor_mul(nrmD[0:1, 514:515], nrmD[0:1, 512:513], nrmD[0:1, 513:514])
            v.tensor_scalar_max(nrmD[0:1, 514:515], nrmD[0:1, 514:515], 1e-30)
        den_free[0] = (ctx.v, ctx.v.tok())
        prod_tok = ctx.v.tok()
        @block.scalar
        def _(s, prod_tok=prod_tok):
            s = ctx.S(s)
            ctx.v.wait(s, prod_tok)
            s.activation(nrmD[0:1, 515:516], nrmD[0:1, 514:515], AF.Ln)
            s.activation(nrmD[0:1, 516:517], nrmD[0:1, 515:516], AF.Exp, scale=0.5)
        B_tok = ctx.s.tok()
        @block.vector
        def _(v, B_tok=B_tok):
            v = ctx.V(v)
            ctx.s.wait(v, B_tok)
            v.tensor_scalar_mul(nrmD[0:1, 517:518], nrmD[0:1, 516:517], -1.0)
        negB_tok = ctx.v.tok()
        @block.tensor
        def _(te, negB_tok=negB_tok):
            ctx.v.wait(te, negB_tok)
            gwait(te, den_free[1])
            ctx.t.inc(te.matmul(ps[7][:, 0:1], onesf1[0:1, 0:128],
                                nrmD[0:1, 517:518], start=True, stop=True))
        bb_tok = ctx.t.tok()
        @block.vector
        def _(v, bb_tok=bb_tok):
            v = ctx.V(v)
            ctx.t.wait(v, bb_tok)
            v.tensor_copy(ebiasD[:], ps[7][:, 0:1])
        den_free[1] = (ctx.v, ctx.v.tok())
        ebias_tok = ctx.v.tok()
        for tc in range(4):
            po = tc % 2
            nkt = 4 * (tc + 1)
            exp_toksD = {}

            def emit_scmmD(kt, b=b, tc=tc, po=po, t0=t0):
                pk = kt % 4
                @block.tensor
                def _(te, b=b, tc=tc, kt=kt, pk=pk, po=po, t0=t0):
                    gwait(te, sc_free[pk])
                    if tc == 0 and kt in (2, 3):
                        ctx.v.wait(te, ebias_tok)
                    if kt == 0:
                        gwait(te, oT_free[po])
                        gwait(te, den_free[po])
                    ctx.t.inc(te.matmul(scbD[pk][:],
                                        KTm[:, t0 + kt * 128: t0 + (kt + 1) * 128],
                                        QTm[:, t0 + tc * 512: t0 + (tc + 1) * 512],
                                        start=True, stop=True))
                exp_toksD[("mm", kt)] = ctx.t.tok()

            def emit_expD_pair(k0, k1, tc=tc, ebias_tok=ebias_tok):
                @block.scalar
                def _(s, k0=k0, k1=k1):
                    s = ctx.S(s)
                    ctx.t.wait(s, exp_toksD[("mm", k1)])
                    ctx.v.wait(s, ebias_tok)
                    for kk in (k0, k1):
                        pk = kk % 4
                        gwait(s, expb_free[pk])
                        s.activation(expbD[pk][:], scbD[pk][:], AF.Exp,
                                     bias=ebiasD[:, 0:1])
                        sc_free[pk] = (ctx.s, ctx.s.tok())
                        exp_toksD[kk] = (ctx.s, ctx.s.tok())
                for kk in (k0, k1):
                    pk = kk % 4
                    j = kk - 4 * tc
                    if j >= 0:
                        @block.vector
                        def _(v, pk=pk, j=j, kk=kk):
                            v = ctx.V(v)
                            exp_toksD[kk][0].wait(v, exp_toksD[kk][1])
                            v.tensor_mul(expbD[pk][:], expbD[pk][:],
                                         cmkb[:, j * 512:(j + 1) * 512])
                        exp_toksD[kk] = (ctx.v, ctx.v.tok())

            def emit_pvD(kt, po=po, t0=t0, nkt=nkt):
                pk = kt % 4
                @block.tensor
                def _(te, kt=kt, pk=pk, po=po, nkt=nkt, t0=t0):
                    exp_toksD[kt][0].wait(te, exp_toksD[kt][1])
                    te.matmul(ps[2 + po][:], Vtm[:, t0 + kt * 128: t0 + (kt + 1) * 128],
                              expbD[pk][:], start=(kt == 0), stop=(kt == nkt - 1))
                    ins = te.matmul(ps[4 + po][:], onesb[:, 0:128],
                                    expbD[pk][:], start=(kt == 0), stop=(kt == nkt - 1))
                    ctx.t.inc(ins)
                expb_free[pk] = (ctx.t, ctx.t.tok())

            for kt in range(min(4, nkt)):
                emit_scmmD(kt)
            emit_expD_pair(0, 1)
            for p2 in range(nkt // 2):
                k0 = 2 * p2
                if k0 + 4 < nkt:
                    emit_scmmD(k0 + 4)
                if k0 + 5 < nkt:
                    emit_scmmD(k0 + 5)
                if k0 + 2 < nkt:
                    emit_expD_pair(k0 + 2, k0 + 3)
                emit_pvD(k0)
                emit_pvD(k0 + 1)
            pv_tok = ctx.t.tok()
            @block.vector
            def _(v, tc=tc, po=po, t0=t0, pv_tok=pv_tok):
                v = ctx.V(v)
                ctx.t.wait(v, pv_tok)
                v.reciprocal(drecD[:], ps[4 + po][:])
                v.tensor_mul(oTd[:, t0 + tc * 512: t0 + (tc + 1) * 512],
                             ps[2 + po][:], drecD[:])
            den_free[po] = (ctx.v, ctx.v.tok())
            oT_free[po] = (ctx.v, ctx.v.tok())
    emit_phase_barrier(ctx)

    xf_ld = [ctx.dsem("Dxf"), ctx.dsem("Dxf")]
    hs_st = [ctx.gst[2], ctx.gst[3]]
    bc2_g = None
    opmix_g = [None, None]
    op_ev = []
    blk_ev = None
    for blk in range(8):
        r0 = blk * 512
        @block.sync
        def _(q, blk=blk, r0=r0):
            if blk >= 1:
                ctx.v.wait(q)
            for half in range(4):
                dstb = (xbuf if half % 2 == 0 else fbuf)[half // 2]
                xf_ld[blk % 2].inc(q.dma_start(out=dstb[:],
                                               in_=scr_xf[r0 + half * 128:
                                                          r0 + (half + 1) * 128, :]), 16)
        for e in range(8):
            @block.tensor
            def _(te, blk=blk, e=e):
                if e == 0:
                    gwait(te, blk_ev)
                gwait(te, bc2_g)
                ctx.t.inc(te.matmul(ps[6][:], cselb[:, e * 128:(e + 1) * 128],
                                    rwT[0:8, blk * 512:(blk + 1) * 512],
                                    start=True, stop=True))
            bct_tok = ctx.t.tok()
            @block.vector
            def _(v, e=e, bct_tok=bct_tok):
                v = ctx.V(v)
                ctx.t.wait(v, bct_tok)
                v.tensor_copy(bcall[:, e * 512:(e + 1) * 512], ps[6][:])
            bc2_g = (ctx.v, ctx.v.tok())
        bcall_tok = ctx.v.tok()
        for e in range(8):
            pm = e % 2
            @block.vector
            def _(v, blk=blk, e=e, pm=pm, bcall_tok=bcall_tok):
                v = ctx.V(v)
                gwait(v, opmix_g[pm])
                v.tensor_mul(opmix[pm][:], oTd[:, blk * 512:(blk + 1) * 512],
                             bcall[:, e * 512:(e + 1) * 512])
            om_tok = ctx.v.tok()
            @block.tensor
            def _(te, blk=blk, e=e, pm=pm, om_tok=om_tok):
                ctx.v.wait(te, om_tok)
                for tt in range(4):
                    for ch in range(2):
                        ins = te.matmul(ps[tt * 2 + ch][:],
                                        opmix[pm][:, tt * 128:(tt + 1) * 128],
                                        opb[:, e * 1024 + ch * 512:
                                            e * 1024 + (ch + 1) * 512],
                                        start=(e == 0), stop=(e == 7))
                ctx.t.inc(ins)
            opmix_g[pm] = (ctx.t, ctx.t.tok())
        hsmm_tok = ctx.t.tok()
        @block.vector
        def _(v, blk=blk, hsmm_tok=hsmm_tok):
            v = ctx.V(v)
            ctx.t.wait(v, hsmm_tok)
            xf_ld[blk % 2].wait(v)
            hs_st[0].wait(v)
            hs_st[1].wait(v)
            for tt in range(4):
                src2 = (xbuf if tt % 2 == 0 else fbuf)[tt // 2]
                dst = (ybuf if tt % 2 == 0 else obuf)[tt // 2]
                for ch in range(2):
                    v.tensor_add(dst[:, ch * 512:(ch + 1) * 512],
                                 ps[tt * 2 + ch][:], src2[:, ch * 512:(ch + 1) * 512])
                ti = blk * 4 + tt
                v.bn_stats(stats[:, ti * 12:ti * 12 + 6], dst[:, 0:512])
                v.bn_stats(stats[:, ti * 12 + 6:ti * 12 + 12], dst[:, 512:1024])
            op_ev.append(ctx.v.tok())
        blk_ev = (ctx.v, op_ev[-1])
        @block.gpsimd
        def _(g, blk=blk, r0=r0):
            ctx.v.wait(g, op_ev[-1])
            for tt in range(4):
                srcb = (ybuf if tt % 2 == 0 else obuf)[tt // 2]
                hs_st[blk % 2].inc(g.dma_start(out=scr_hs[r0 + tt * 128:
                                                          r0 + (tt + 1) * 128, :],
                                               in_=srcb[:]), 16)
    emit_phase_barrier(ctx, extra_dsems=hs_st)

    st_n2 = weighted_ln(scr_hs, lnw2, lnb2, rn2_sb, stats_ready=True)

    nT_done2 = [None, None]
    gguard2 = [None, None]
    for tc in range(8):
        pn = tc % 2
        @block.sync
        def _(q, tc=tc, pn=pn):
            if tc < 2:
                for p2 in range(2):
                    st_n2[p2].wait(q)
            else:
                ctx.t.wait(q, nT_done2[pn])
            for dc in range(8):
                nTs[pn].inc(q.dma_start(out=nT[pn][:, dc * 512:(dc + 1) * 512],
                                        in_=scr_n[tc * 512:(tc + 1) * 512,
                                                  dc * 128:(dc + 1) * 128],
                                        transpose=True), 16)
        for i in range(4):
            tg = tc * 4 + i
            @block.tensor
            def _(te, tc=tc, pn=pn, i=i):
                nTs[pn].wait(te)
                gwait(te, gguard2[0])
                for dc in range(8):
                    ins = te.matmul(ps[5][:, 0:8],
                                    nT[pn][:, dc * 512 + i * 128: dc * 512 + (i + 1) * 128],
                                    simb2[:, dc * 8:(dc + 1) * 8],
                                    start=(dc == 0), stop=(dc == 7))
                ctx.t.inc(ins)
            gating_tile(ps[5][:, 0:8], rn2_sb[:, tg:tg + 1], rsimb2, sgb2, rw2T, tg,
                        gguard2)
        nT_done2[pn] = ctx.t.tok()
    emit_phase_barrier(ctx)

    ctx.reset_to(m1)
    w2e = [ctx.sbuf("D_w2e%d" % i, [128, 8 * 1024], BF16) for i in range(2)]
    m2 = ctx.mark()
    w1e = [ctx.sbuf("D_w1e%d" % i, [128, 8 * 1024], BF16) for i in range(2)]
    bcmoe = [ctx.sbuf("D_bcm%d" % i, [128, 512], BF16) for i in range(2)]
    isp = [ctx.sbuf("D_isp%d" % i, [128, 512], BF16) for i in range(4)]
    w1sem = [ctx.dsem("Dw1", swdge=True), ctx.dsem("Dw1", swdge=True)]
    w2sem = [ctx.dsem("Dw2", swdge=True), ctx.dsem("Dw2", swdge=True)]
    intsem = [ctx.dsem("Din"), ctx.dsem("Din")]
    isst = [ctx.dsem("Dis", swdge=True) for _ in range(4)]
    hs_ld = [ctx.dsem("Dhl"), ctx.dsem("Dhl")]
    yst = [ctx.gst[2], ctx.gst[3]]
    w1_g = [None, None]
    nT_done3 = [None, None]
    bcm_g = [None, None]
    bcps_g = [None, None]
    isp_g = [None, None, None, None]
    h_g = [None, None]
    for e in range(8):
        pe = e % 2
        @block.gpsimd
        def _(g, e=e, pe=pe):
            gwait(g, w1_g[pe])
            w1sem[pe].inc(g.dma_start(
                out=w1e[pe][:].rearrange("p (dc m) -> p dc m", dc=8),
                in_=W["w1"][e].rearrange("(dc p) m -> p dc m", p=128)), 16)
        for tc in range(8):
            pn = tc % 2
            if e == 0 or True:
                @block.sync
                def _(q, e=e, tc=tc, pn=pn):
                    if e == 0 and tc < 2:
                        pass
                    else:
                        ctx.t.wait(q, nT_done3[pn])
                    for dc in range(8):
                        nTs[pn].inc(q.dma_start(out=nT[pn][:, dc * 512:(dc + 1) * 512],
                                                in_=scr_n[tc * 512:(tc + 1) * 512,
                                                          dc * 128:(dc + 1) * 128],
                                                transpose=True), 16)
            pb2 = tc % 2
            bkm = ps[6] if pb2 == 0 else ps[7]
            @block.tensor
            def _(te, e=e, tc=tc, pb2=pb2, bkm=bkm):
                gwait(te, bcps_g[pb2])
                ctx.t.inc(te.matmul(bkm[:], cselb[:, e * 128:(e + 1) * 128],
                                    rw2T[0:8, tc * 512:(tc + 1) * 512],
                                    start=True, stop=True))
            bcm_tok = ctx.t.tok()
            @block.vector
            def _(v, pb2=pb2, bkm=bkm, bcm_tok=bcm_tok):
                v = ctx.V(v)
                ctx.t.wait(v, bcm_tok)
                gwait(v, bcm_g[pb2])
                v.tensor_copy(bcmoe[pb2][:], bkm[:])
            bcps_g[pb2] = (ctx.v, ctx.v.tok())
            bcm_tok2 = ctx.v.tok()
            gl_toks = {}
            for ic2 in range(4):
                i0, i1 = 2 * ic2, 2 * ic2 + 1
                for ic in (i0, i1):
                    ph = ic % 2
                    @block.tensor
                    def _(te, e=e, tc=tc, pn=pn, ic=ic, ph=ph):
                        if tc == 0 and ic == 0:
                            w1sem[e % 2].wait(te)
                        nTs[pn].wait(te)
                        gwait(te, h_g[ph])
                        for dc in range(8):
                            ins = te.matmul(ps[0 + ph][:],
                                            w1e[e % 2][:, dc * 1024 + ic * 128:
                                                       dc * 1024 + (ic + 1) * 128],
                                            nT[pn][:, dc * 512:(dc + 1) * 512],
                                            start=(dc == 0), stop=(dc == 7))
                        ctx.t.inc(ins)
                h_tok = ctx.t.tok()
                @block.scalar
                def _(s, i0=i0, i1=i1, h_tok=h_tok):
                    s = ctx.S(s)
                    ctx.t.wait(s, h_tok)
                    for ic in (i0, i1):
                        ph = ic % 2
                        sl = ic % 4
                        gwait(s, isp_g[sl])
                        isst[sl].wait(s)
                        s.activation(isp[sl][:], ps[0 + ph][:], AF.Gelu)
                        h_g[ph] = (ctx.s, ctx.s.tok())
                        gl_toks[ic] = ctx.s.tok()
                for ic in (i0, i1):
                    sl = ic % 4
                    @block.vector
                    def _(v, ic=ic, sl=sl, pb2=pb2, bcm_tok2=bcm_tok2):
                        v = ctx.V(v)
                        ctx.s.wait(v, gl_toks[ic])
                        isst[sl].wait(v)
                        v.tensor_mul(isp[sl][:], isp[sl][:], bcmoe[pb2][:])
                    is_tok = ctx.v.tok()
                    isp_g[sl] = (ctx.v, is_tok)
                    @block.gpsimd
                    def _(g, e=e, tc=tc, ic=ic, sl=sl, is_tok=is_tok):
                        ctx.v.wait(g, is_tok)
                        isst[sl].inc(g.dma_start(
                            out=scr_int[e * 1024 + ic * 128: e * 1024 + (ic + 1) * 128,
                                        tc * 512:(tc + 1) * 512],
                            in_=isp[sl][:]), 16)
            bcm_g[pb2] = (ctx.v, ctx.v.tok())
            nT_done3[pn] = ctx.t.tok()
        w1_g[pe] = (ctx.t, ctx.t.tok())
    @block.sync
    def _(q):
        for p in range(4):
            isst[p].wait(q)
    emit_phase_barrier(ctx, extra_dsems=isst)

    ctx.reset_to(m2)
    intS = [ctx.sbuf("D_intS%d" % i, [128, 1024], BF16) for i in range(16)]
    outacc = ctx.sbuf("D_oacc", [128, 8 * 1024], F32)
    int_g = [None, None]
    w2_g = [None, None]
    oa_g = None
    op2_g = [None, None]
    for sb in range(4):
        tb0 = sb * 1024
        y_toks = []
        for e in range(8):
            pe = e % 2
            @block.gpsimd
            def _(g, e=e, pe=pe, sb=sb):
                gwait(g, w2_g[pe])
                w2sem[pe].inc(g.dma_start(
                    out=w2e[pe][:].rearrange("p (ic m) -> p ic m", ic=8),
                    in_=W["w2"][e].rearrange("(ic p) m -> p ic m", p=128)), 16)
            @block.sync
            def _(q, e=e, pe=pe, sb=sb, tb0=tb0):
                gwait(q, int_g[pe])
                for ic in range(8):
                    intsem[pe].inc(q.dma_start(
                        out=intS[pe * 8 + ic][:],
                        in_=scr_int[e * 1024 + ic * 128: e * 1024 + (ic + 1) * 128,
                                    tb0:tb0 + 1024]), 16)
            for tt in range(8):
                pt = tt % 2
                @block.tensor
                def _(te, e=e, pe=pe, tt=tt, pt=pt):
                    if tt == 0:
                        w2sem[pe].wait(te)
                        intsem[pe].wait(te)
                    gwait(te, op2_g[pt])
                    for ch in range(2):
                        for ic in range(8):
                            ins = te.matmul(ps[pt * 2 + ch][:],
                                            intS[pe * 8 + ic][:, tt * 128:(tt + 1) * 128],
                                            w2e[pe][:, ic * 1024 + ch * 512:
                                                   ic * 1024 + (ch + 1) * 512],
                                            start=(ic == 0), stop=(ic == 7))
                        ctx.t.inc(ins)
                mm_tok = ctx.t.tok()
                @block.vector
                def _(v, e=e, tt=tt, pt=pt, mm_tok=mm_tok):
                    v = ctx.V(v)
                    ctx.t.wait(v, mm_tok)
                    gwait(v, oa_g)
                    for ch in range(2):
                        dst = outacc[:, tt * 1024 + ch * 512: tt * 1024 + (ch + 1) * 512]
                        if e == 0:
                            v.tensor_copy(dst, ps[pt * 2 + ch][:])
                        else:
                            v.tensor_add(dst, dst, ps[pt * 2 + ch][:])
                op2_g[pt] = (ctx.v, ctx.v.tok())
                if e == 7:
                    @block.sync
                    def _(q, sb=sb, tt=tt, tb0=tb0):
                        if tt >= 2:
                            ctx.v.wait(q, y_toks[tt - 2])
                        else:
                            ctx.v.wait(q)
                        hs_ld[tt % 2].inc(q.dma_start(
                            out=xbuf[tt % 2][:],
                            in_=scr_hs[tb0 + tt * 128: tb0 + (tt + 1) * 128, :]), 16)
                    @block.vector
                    def _(v, sb=sb, tt=tt):
                        v = ctx.V(v)
                        hs_ld[tt % 2].wait(v)
                        yst[tt % 2].wait(v)
                        v.tensor_add(ybuf[tt % 2][:],
                                     outacc[:, tt * 1024:(tt + 1) * 1024],
                                     xbuf[tt % 2][:])
                        ti = sb * 8 + tt
                        v.bn_stats(stats[:, ti * 12:ti * 12 + 6],
                                   ybuf[tt % 2][:, 0:512])
                        v.bn_stats(stats[:, ti * 12 + 6:ti * 12 + 12],
                                   ybuf[tt % 2][:, 512:1024])
                        y_toks.append(ctx.v.tok())
                    @block.gpsimd
                    def _(g, sb=sb, tt=tt, tb0=tb0):
                        ctx.v.wait(g, y_toks[-1])
                        yst[tt % 2].inc(g.dma_start(
                            out=scr_y[tb0 + tt * 128: tb0 + (tt + 1) * 128, :],
                            in_=ybuf[tt % 2][:]), 16)
            w2_g[pe] = (ctx.t, ctx.t.tok())
            int_g[pe] = (ctx.t, ctx.t.tok())
        oa_g = (ctx.v, ctx.v.tok())

    if do_epi:
        @block.sync
        def _(q):
            for p in range(2):
                yst[p].wait(q)
        return emit_ln_residual_epilogue(ctx, scr_y, x_hbm, out_hbm, mean, var,
                                         rstd, xbuf, ybuf, obuf, stats,
                                         stats_ready=True, out_stats=out_stats)
    return None


def build_csel():
    sel = np.zeros((8, 1024), dtype=np.float32)
    for e in range(8):
        sel[e, e * 128:(e + 1) * 128] = 1.0
    return sel


def build_cmk():
    m = np.zeros((128, 4 * 512), dtype=np.float32)
    p = np.arange(128)[:, None]
    f = np.arange(512)[None, :]
    for j in range(4):
        m[:, j * 512:(j + 1) * 512] = (f >= p + j * 128 - 0).astype(np.float32) if False \
            else ((j * 128 + p) <= f).astype(np.float32)
    return m


def build_phaseD_program(**kw):
    nc = bass.Bass("TRN2", target_bir_lowering=False, debug=False)
    x = nc.dram_tensor("x", [TOK, D], F32, kind="ExternalInput").ap()
    names = [("ln1_w", [D]), ("ln1_b", [D]), ("ln2_w", [D]), ("ln2_b", [D]),
             ("smha_sim", [D, 8]), ("smha_gates", [8]),
             ("q_proj", [8, D, 128]), ("k_proj", [8, D, 128]), ("v_proj", [8, D, 128]),
             ("o_proj", [8, 128, D]), ("moe_sim", [D, 8]), ("moe_gates", [8]),
             ("w1", [8, D, D]), ("w2", [8, D, D])]
    W = {nm: nc.dram_tensor(nm, sh, F32, kind="ExternalInput").ap() for nm, sh in names}
    ident = nc.dram_tensor("ident", [128, 128], F32, kind="ExternalInput").ap()
    identb_i = nc.dram_tensor("identb", [128, 128], BF16, kind="ExternalInput").ap()
    onesi = nc.dram_tensor("onesi", [128, 128], BF16, kind="ExternalInput").ap()
    onesf = nc.dram_tensor("onesf", [1, 128], F32, kind="ExternalInput").ap()
    cseli = nc.dram_tensor("cseli", [8, 1024], BF16, kind="ExternalInput").ap()
    cmki = nc.dram_tensor("cmki", [128, 2048], BF16, kind="ExternalInput").ap()
    out = nc.dram_tensor("out", [TOK, D], F32, kind="ExternalOutput").ap()
    scr_n = nc.dram_tensor("scr_n", [TOK, D], BF16, kind="Internal").ap()
    scr_y = nc.dram_tensor("scr_y", [TOK, D], F32, kind="Internal").ap()
    scr_xf = nc.dram_tensor("scr_xf", [TOK, D], F32, kind="Internal").ap()
    scr_hs = nc.dram_tensor("scr_hs", [TOK, D], F32, kind="Internal").ap()
    scr_int = nc.dram_tensor("scr_int", [8 * 1024, TOK], BF16, kind="Internal").ap()
    with nc.Block() as block:
        ctx = Ctx(nc, block)
        ctx.ident_hbm = ident
        ctx.identb_hbm = identb_i
        ctx.ones_hbm = onesi
        ctx.onesf_hbm = onesf
        ctx.csel_hbm = cseli
        ctx.cmk_hbm = cmki
        st = emit_phaseD(ctx, x, W, out, scr_n, scr_y, scr_xf, scr_hs, scr_int, **kw)
        @block.sync
        def _(q):
            if st is not None:
                for p in range(2):
                    st[p].wait(q)
            ctx.v.wait(q)
            ctx.s.wait(q)
            ctx.t.wait(q)
    return nc


def build_ident():
    return np.eye(128, dtype=np.float32)


def build_phaseB_program(**kw):
    nc = bass.Bass("TRN2", target_bir_lowering=False, debug=False)
    x = nc.dram_tensor("x", [TOK, D], F32, kind="ExternalInput").ap()
    w1 = nc.dram_tensor("w1", [D, 4096], F32, kind="ExternalInput").ap()
    b1 = nc.dram_tensor("b1", [4096], F32, kind="ExternalInput").ap()
    w2 = nc.dram_tensor("w2", [4096, D], F32, kind="ExternalInput").ap()
    b2 = nc.dram_tensor("b2", [D], F32, kind="ExternalInput").ap()
    ident = nc.dram_tensor("ident", [128, 128], F32, kind="ExternalInput").ap()
    out = nc.dram_tensor("out", [TOK, D], F32, kind="ExternalOutput").ap()
    scr_n = nc.dram_tensor("scr_n", [TOK, D], BF16, kind="Internal").ap()
    scr_y = nc.dram_tensor("scr_y", [TOK, D], F32, kind="Internal").ap()
    with nc.Block() as block:
        ctx = Ctx(nc, block)
        ctx.ident_hbm = ident
        st = emit_phaseB(ctx, x, w1, b1, w2, b2, out, scr_n, scr_y, **kw)
        @block.sync
        def _(q):
            if st is not None:
                for p in range(2):
                    st[p].wait(q)
            ctx.v.wait(q)
            ctx.s.wait(q)
            ctx.t.wait(q)
    return nc


def build_full_program():
    nc = bass.Bass("TRN2", target_bir_lowering=False, debug=False)
    x = nc.dram_tensor("x", [TOK, D], F32, kind="ExternalInput").ap()
    out = nc.dram_tensor("out", [TOK, D], F32, kind="ExternalOutput").ap()
    namesA = [("W_DQ", [D, 32]), ("b_DQ", [32]), ("W_UQ", [32, D]), ("b_UQ", [D]),
              ("W_DKV", [D, 32]), ("b_DKV", [32]), ("W_UK", [32, D]), ("b_UK", [D]),
              ("W_UV", [32, D]), ("b_UV", [D]), ("W_QR", [32, 256]), ("b_QR", [256]),
              ("W_KR", [D, 32]), ("b_KR", [32]), ("W_O", [D, D]), ("b_O", [D])]
    WA = {nm: nc.dram_tensor("A_" + nm, sh, F32, kind="ExternalInput").ap()
          for nm, sh in namesA}
    w1 = nc.dram_tensor("w1", [D, 4096], F32, kind="ExternalInput").ap()
    b1 = nc.dram_tensor("b1", [4096], F32, kind="ExternalInput").ap()
    w2 = nc.dram_tensor("w2", [4096, D], F32, kind="ExternalInput").ap()
    b2 = nc.dram_tensor("b2", [D], F32, kind="ExternalInput").ap()
    namesC = [("Wq", [D, 128]), ("bq", [128]), ("Wk", [D, 128]), ("bk", [128]),
              ("Wv", [D, 128]), ("bv", [128]), ("Wab", [D, 2]), ("bab", [2]),
              ("Wg", [D, 128]), ("bg", [128]), ("Wo", [128, D]), ("bo", [D])]
    WC = {nm: nc.dram_tensor("C_" + nm, sh, F32, kind="ExternalInput").ap()
          for nm, sh in namesC}
    namesD = [("ln1_w", [D]), ("ln1_b", [D]), ("ln2_w", [D]), ("ln2_b", [D]),
              ("smha_sim", [D, 8]), ("smha_gates", [8]),
              ("q_proj", [8, D, 128]), ("k_proj", [8, D, 128]), ("v_proj", [8, D, 128]),
              ("o_proj", [8, 128, D]), ("moe_sim", [D, 8]), ("moe_gates", [8]),
              ("w1", [8, D, D]), ("w2", [8, D, D])]
    WD = {nm: nc.dram_tensor("D_" + nm, sh, F32, kind="ExternalInput").ap()
          for nm, sh in namesD}
    ident = nc.dram_tensor("ident", [128, 128], F32, kind="ExternalInput").ap()
    identb_i = nc.dram_tensor("identb", [128, 128], BF16, kind="ExternalInput").ap()
    onesi = nc.dram_tensor("onesi", [128, 128], BF16, kind="ExternalInput").ap()
    onesf = nc.dram_tensor("onesf", [1, 128], F32, kind="ExternalInput").ap()
    cseli = nc.dram_tensor("cseli", [8, 1024], BF16, kind="ExternalInput").ap()
    cmki = nc.dram_tensor("cmki", [128, 2048], BF16, kind="ExternalInput").ap()
    ccum = nc.dram_tensor("ccum", [128, 128], F32, kind="ExternalInput").ap()
    csuf = nc.dram_tensor("csuf", [128, 128], F32, kind="ExternalInput").ap()
    cmsl = nc.dram_tensor("cmsl", [128, 128], F32, kind="ExternalInput").ap()
    cmui = nc.dram_tensor("cmui", [128, 128], F32, kind="ExternalInput").ap()
    x1 = nc.dram_tensor("x1", [TOK, D], F32, kind="Internal").ap()
    x2 = nc.dram_tensor("x2", [TOK, D], F32, kind="Internal").ap()
    x3 = nc.dram_tensor("x3", [TOK, D], F32, kind="Internal").ap()
    scr_n = nc.dram_tensor("scr_n", [TOK, D], BF16, kind="Internal").ap()
    scr_y = nc.dram_tensor("scr_y", [TOK, D], F32, kind="Internal").ap()
    scr_xf = nc.dram_tensor("scr_xf", [TOK, D], F32, kind="Internal").ap()
    scr_hs = nc.dram_tensor("scr_hs", [TOK, D], F32, kind="Internal").ap()
    scr_int = nc.dram_tensor("scr_int", [8 * 1024, TOK], BF16, kind="Internal").ap()
    with nc.Block() as block:
        ctx = Ctx(nc, block)
        ctx.ident_hbm = ident
        ctx.identb_hbm = identb_i
        ctx.ones_hbm = onesi
        ctx.onesf_hbm = onesf
        ctx.csel_hbm = cseli
        ctx.cmk_hbm = cmki
        ctx.ccum_hbm = ccum
        ctx.csuf_hbm = csuf
        ctx.cmsl_hbm = cmsl
        ctx.cmui_hbm = cmui
        hstats = ctx.sbuf("hstats", [128, NTILE * 14], F32, perm=True)
        stA = emit_phaseA(ctx, x, WA, x1, scr_n, scr_y, out_stats=hstats)
        emit_phase_barrier(ctx, extra_dsems=stA)
        ctx.phase_reset()
        stB = emit_phaseB(ctx, x1, w1, b1, w2, b2, x2, scr_n, scr_y,
                          in_stats=hstats, out_stats=hstats)
        emit_phase_barrier(ctx, extra_dsems=stB)
        ctx.phase_reset()
        stC = emit_phaseC(ctx, x2, WC, x3, scr_n, scr_y,
                          in_stats=hstats, out_stats=hstats)
        emit_phase_barrier(ctx, extra_dsems=stC)
        ctx.phase_reset()
        stD = emit_phaseD(ctx, x3, WD, out, scr_n, scr_y, scr_xf, scr_hs, scr_int,
                          in_stats=hstats)
        @block.sync
        def _(q):
            for p in range(2):
                stD[p].wait(q)
            ctx.v.wait(q)
            ctx.s.wait(q)
            ctx.t.wait(q)
            ctx.g.wait(q)
    return nc


def _cast_consts():
    import ml_dtypes
    return {
        "ident": build_ident(),
        "identb": np.eye(128, dtype=np.float32).astype(ml_dtypes.bfloat16),
        "onesi": np.ones((128, 128), np.float32).astype(ml_dtypes.bfloat16),
        "onesf": np.ones((1, 128), np.float32),
        "cseli": build_csel().astype(ml_dtypes.bfloat16),
        "cmki": build_cmk().astype(ml_dtypes.bfloat16),
    }


_PROG_CACHE = {}


def _get_prog(name, builder):
    if name not in _PROG_CACHE:
        _PROG_CACHE[name] = builder()
    return _PROG_CACHE[name]


def kernel(**inputs):
    import os
    if os.environ.get("MK_UNFUSED"):
        return _kernel_multi(**inputs)
    C = _cast_consts()
    ccum, csuf, cmsl, cmui = build_cmats()
    xs = [np.ascontiguousarray(inputs["L0_attn_h_t"][2 * c:2 * c + 2]
                               .reshape(TOK, D)) for c in range(NCORES)]
    nc = _get_prog("full", build_full_program)
    WA = {"A_" + k[len("L0_attn_"):]: inputs[k] for k in inputs
          if k.startswith("L0_attn_") and k != "L0_attn_h_t"}
    Wab = np.ascontiguousarray(np.concatenate([inputs["L1_attn_Wa"],
                                               inputs["L1_attn_Wb"]], axis=1))
    bab = np.ascontiguousarray(np.concatenate([inputs["L1_attn_ba"],
                                               inputs["L1_attn_bb"]]))
    WC = {"C_Wq": inputs["L1_attn_Wq"], "C_bq": inputs["L1_attn_bq"],
          "C_Wk": inputs["L1_attn_Wk"], "C_bk": inputs["L1_attn_bk"],
          "C_Wv": inputs["L1_attn_Wv"], "C_bv": inputs["L1_attn_bv"],
          "C_Wab": Wab, "C_bab": bab,
          "C_Wg": inputs["L1_attn_Wg"], "C_bg": inputs["L1_attn_bg"],
          "C_Wo": inputs["L1_attn_Wo"], "C_bo": inputs["L1_attn_bo"]}
    WD = {"D_" + k[len("L1_ffn_"):]: inputs[k] for k in inputs
          if k.startswith("L1_ffn_")}
    maps = []
    for c in range(NCORES):
        m = dict(x=xs[c],
                 w1=np.ascontiguousarray(inputs["L0_ffn_w1"][c]),
                 b1=np.ascontiguousarray(inputs["L0_ffn_b1"][c]),
                 w2=np.ascontiguousarray(inputs["L0_ffn_w2"][c]),
                 b2=np.ascontiguousarray(inputs["L0_ffn_b2"][c]),
                 ident=C["ident"], identb=C["identb"], onesi=C["onesi"],
                 onesf=C["onesf"], cseli=C["cseli"], cmki=C["cmki"],
                 ccum=ccum, csuf=csuf, cmsl=cmsl, cmui=cmui)
        m.update(WA)
        m.update(WC)
        m.update(WD)
        maps.append(m)
    res = run_bass_kernel_spmd(nc, maps, core_ids=list(range(NCORES)))
    xs = [res.results[c]["out"] for c in range(NCORES)]
    return np.concatenate([x.reshape(2, T_SEQ, D) for x in xs], axis=0)


def _kernel_multi(**inputs):
    C = _cast_consts()
    ccum, csuf, cmsl, cmui = build_cmats()
    xs = [np.ascontiguousarray(inputs["L0_attn_h_t"][2 * c:2 * c + 2]
                               .reshape(TOK, D)) for c in range(NCORES)]

    ncA = _get_prog("A", build_phaseA_program)
    WA = {k[len("L0_attn_"):]: inputs[k] for k in inputs if k.startswith("L0_attn_")
          and k != "L0_attn_h_t"}
    mapsA = [dict(x=xs[c], ident=C["ident"], onesi=C["onesi"], onesf=C["onesf"],
                  **WA) for c in range(NCORES)]
    resA = run_bass_kernel_spmd(ncA, mapsA, core_ids=list(range(NCORES)))
    xs = [resA.results[c]["out"] for c in range(NCORES)]

    ncB = _get_prog("B", build_phaseB_program)
    mapsB = [dict(x=xs[c], w1=np.ascontiguousarray(inputs["L0_ffn_w1"][c]),
                  b1=np.ascontiguousarray(inputs["L0_ffn_b1"][c]),
                  w2=np.ascontiguousarray(inputs["L0_ffn_w2"][c]),
                  b2=np.ascontiguousarray(inputs["L0_ffn_b2"][c]),
                  ident=C["ident"]) for c in range(NCORES)]
    resB = run_bass_kernel_spmd(ncB, mapsB, core_ids=list(range(NCORES)))
    xs = [resB.results[c]["out"] for c in range(NCORES)]

    ncC = _get_prog("C", build_phaseC_program)
    Wab = np.ascontiguousarray(np.concatenate([inputs["L1_attn_Wa"],
                                               inputs["L1_attn_Wb"]], axis=1))
    bab = np.ascontiguousarray(np.concatenate([inputs["L1_attn_ba"],
                                               inputs["L1_attn_bb"]]))
    WC = dict(Wq=inputs["L1_attn_Wq"], bq=inputs["L1_attn_bq"],
              Wk=inputs["L1_attn_Wk"], bk=inputs["L1_attn_bk"],
              Wv=inputs["L1_attn_Wv"], bv=inputs["L1_attn_bv"],
              Wab=Wab, bab=bab,
              Wg=inputs["L1_attn_Wg"], bg=inputs["L1_attn_bg"],
              Wo=inputs["L1_attn_Wo"], bo=inputs["L1_attn_bo"])
    mapsC = [dict(x=xs[c], ident=C["ident"], onesf=C["onesf"],
                  ccum=ccum, csuf=csuf, cmsl=cmsl, cmui=cmui, **WC)
             for c in range(NCORES)]
    resC = run_bass_kernel_spmd(ncC, mapsC, core_ids=list(range(NCORES)))
    xs = [resC.results[c]["out"] for c in range(NCORES)]

    ncD = _get_prog("D", build_phaseD_program)
    WD = {k[len("L1_ffn_"):]: inputs[k] for k in inputs if k.startswith("L1_ffn_")}
    mapsD = [dict(x=xs[c], ident=C["ident"], identb=C["identb"], onesi=C["onesi"],
                  onesf=C["onesf"], cseli=C["cseli"], cmki=C["cmki"], **WD)
             for c in range(NCORES)]
    resD = run_bass_kernel_spmd(ncD, mapsD, core_ids=list(range(NCORES)))
    xs = [resD.results[c]["out"] for c in range(NCORES)]

    return np.concatenate([x.reshape(2, T_SEQ, D) for x in xs], axis=0)


def build_ones():
    return np.ones((128, 128), dtype=np.float32)


def build_phaseA_program(**kw):
    nc = bass.Bass("TRN2", target_bir_lowering=False, debug=False)
    x = nc.dram_tensor("x", [TOK, D], F32, kind="ExternalInput").ap()
    names = [("W_DQ", [D, 32]), ("b_DQ", [32]), ("W_UQ", [32, D]), ("b_UQ", [D]),
             ("W_DKV", [D, 32]), ("b_DKV", [32]), ("W_UK", [32, D]), ("b_UK", [D]),
             ("W_UV", [32, D]), ("b_UV", [D]), ("W_QR", [32, 256]), ("b_QR", [256]),
             ("W_KR", [D, 32]), ("b_KR", [32]), ("W_O", [D, D]), ("b_O", [D])]
    W = {nm: nc.dram_tensor(nm, sh, F32, kind="ExternalInput").ap() for nm, sh in names}
    ident = nc.dram_tensor("ident", [128, 128], F32, kind="ExternalInput").ap()
    onesi = nc.dram_tensor("onesi", [128, 128], BF16, kind="ExternalInput").ap()
    onesf = nc.dram_tensor("onesf", [1, 128], F32, kind="ExternalInput").ap()
    out = nc.dram_tensor("out", [TOK, D], F32, kind="ExternalOutput").ap()
    scr_n = nc.dram_tensor("scr_n", [TOK, D], BF16, kind="Internal").ap()
    scr_y = nc.dram_tensor("scr_y", [TOK, D], F32, kind="Internal").ap()
    with nc.Block() as block:
        ctx = Ctx(nc, block)
        ctx.ident_hbm = ident
        ctx.ones_hbm = onesi
        ctx.onesf_hbm = onesf
        st = emit_phaseA(ctx, x, W, out, scr_n, scr_y, **kw)
        @block.sync
        def _(q):
            if st is not None:
                for p in range(2):
                    st[p].wait(q)
            ctx.v.wait(q)
            ctx.s.wait(q)
            ctx.t.wait(q)
    return nc
